# version 02
# baseline (speedup 1.0000x reference)
import numpy as np
import ml_dtypes
import concourse.bass as bass
import concourse.mybir as mybir
from concourse.bass_utils import run_bass_kernel_spmd
from contextlib import ExitStack

F32 = mybir.dt.float32
BF16 = mybir.dt.bfloat16
I32 = mybir.dt.int32
AF = mybir.ActivationFunctionType
ALU = mybir.AluOpType
AX = mybir.AxisListType

B, S, D = 4, 4096, 1024
NT = B * S
NC = 8
BF = ml_dtypes.bfloat16

ENGS = ("te", "ve", "se", "ge", "sy")


class Ctx:
    pass


class Prog:

    def __init__(self):
        self.q = {e: [] for e in ENGS}
        self.cnt = {}
        self.hwm = {e: {} for e in ENGS}
        self.semnames = []

    def sem(self, name):
        if name not in self.cnt:
            self.cnt[name] = 0
            self.semnames.append(name)
        return name

    def wait(self, eng, sem, val=None):
        v = self.cnt.get(sem, 0) if val is None else val
        if v <= 0:
            return
        if self.hwm[eng].get(sem, 0) >= v:
            return
        self.hwm[eng][sem] = v
        self.q[eng].append(("w", sem, v))

    def emit(self, eng, fn, inc=None, inc_by=1):
        if inc is not None:
            self.sem(inc)
        self.q[eng].append(("e", fn, inc, inc_by))
        if inc is not None:
            self.cnt[inc] += inc_by
            return (inc, self.cnt[inc])
        return None

    def tok(self, sem):
        return (sem, self.cnt.get(sem, 0))

    def dep(self, eng, tok):
        if tok is not None:
            self.wait(eng, tok[0], tok[1])

    def build(self, nc, sbufs, psums, c):
        with ExitStack() as es:
            c.sb = {}
            for name, (shape, dt) in sbufs.items():
                c.sb[name] = es.enter_context(nc.sbuf_tensor(name, list(shape), dt))
            c.ps = {}
            for name, (shape, dt) in psums.items():
                c.ps[name] = es.enter_context(nc.psum_tensor(name, list(shape), dt))
            sems = {name: es.enter_context(nc.semaphore(name)) for name in self.semnames}
            block = es.enter_context(nc.Block())

            def section(eng_name):
                items = self.q[eng_name]

                def body(E):
                    for it in items:
                        if it[0] == "w":
                            E.wait_ge(sems[it[1]], it[2])
                        else:
                            _, fn, inc, inc_by = it
                            inst = fn(E, c)
                            if inc is not None:
                                inst.then_inc(sems[inc], inc_by)
                body.__name__ = "_"
                return body

            block.sync(section("sy"))
            block.tensor(section("te"))
            block.vector(section("ve"))
            block.scalar(section("se"))
            block.gpsimd(section("ge"))
        return nc


def build_A1():
    nc = bass.Bass("TRN2", target_bir_lowering=False, debug=False)
    p = Prog()
    c = Ctx()
    NTILE = NT // 128

    dram = {}
    dram["xfull"] = nc.dram_tensor("xfull", [NT, D], F32, kind="ExternalInput").ap()
    dram["xsl"] = nc.dram_tensor("xsl", [B * 512, D], F32, kind="ExternalInput").ap()
    dram["isel"] = nc.dram_tensor("isel", [S, 512], BF16, kind="ExternalInput").ap()
    dram["x1sl"] = nc.dram_tensor("x1sl", [B * 512, D], F32, kind="ExternalOutput").ap()
    dram["ln0d"] = nc.dram_tensor("ln0d", [NT, D], BF16).ap()
    c.dram = dram

    sbufs = {
        "xin": ((128, 2 * D), F32),
        "stat": ((128, 2 * NTILE), F32),
        "mu": ((128, NTILE), F32),
        "rr": ((128, NTILE), F32),
        "hsum": ((128, 2 * 64), F32),
        "mtm": ((128, 32 * 64), F32),
        "mtm_b": ((128, 32 * 64), BF16),
        "sq": ((128, 2 * D), F32),
        "lnt": ((128, 2 * D), BF16),
        "mfm_b": ((64, S), BF16),
        "msl_b": ((64, 512), BF16),
        "ident": ((128, 128), F32),
        "onesc": ((128, 1), BF16),
        "iselt": ((128, 2 * 512), BF16),
        "strip": ((128, 32 * 512), BF16),
        "sf32": ((128, 2 * 512), F32),
        "zrow": ((1, 512), F32),
        "zone": ((1, 1), BF16),
        "zcol": ((128, 4), F32),
        "lnj": ((128, 2 * D), BF16),
        "xres": ((128, 2 * D), F32),
        "outt": ((128, 2 * D), F32),
    }
    psums = {
        "P": ((128, 4096), F32),
    }

    for s_ in ("dy", "dg", "t", "v", "s", "g"):
        p.sem(s_)

    for i in range(NTILE):
        bo = (i % 2) * D
        tin = p.emit("sy", lambda E, c, i=i, bo=bo: E.dma_start(
            out=c.sb["xin"][:, bo:bo + D], in_=c.dram["xfull"][128 * i:128 * (i + 1), :]),
            inc="dy", inc_by=16)
        p.dep("ve", tin)
        p.emit("ve", lambda E, c, i=i, bo=bo: E.reduce_sum(
            out=c.sb["stat"][:, i:i + 1], in_=c.sb["xin"][:, bo:bo + D], axis=AX.X), inc="v")
        p.dep("se", tin)
        p.emit("se", lambda E, c, i=i, bo=bo: E.activation(
            out=c.sb["sq"][:, bo:bo + D], in_=c.sb["xin"][:, bo:bo + D], func=AF.Square,
            accum_out=c.sb["stat"][:, NTILE + i:NTILE + i + 1]), inc="s")
        if i >= 1:
            p.wait("sy", "v")
            p.wait("sy", "s")

    p.wait("ve", "v")
    p.wait("ve", "s")
    t1 = p.emit("ve", lambda E, c: E.tensor_scalar_mul(
        out=c.sb["mu"][:], in0=c.sb["stat"][:, 0:NTILE], scalar1=1.0 / D), inc="v")
    p.dep("ve", t1)
    t2 = p.emit("ve", lambda E, c: E.tensor_tensor(
        out=c.sb["rr"][:], in0=c.sb["mu"][:], in1=c.sb["mu"][:], op=ALU.mult), inc="v")
    p.dep("ve", t2)
    t3 = p.emit("ve", lambda E, c: E.scalar_tensor_tensor(
        out=c.sb["rr"][:], in0=c.sb["stat"][:, NTILE:2 * NTILE], scalar=1.0 / D,
        in1=c.sb["rr"][:], op0=ALU.mult, op1=ALU.subtract), inc="v")
    p.dep("ve", t3)
    t4 = p.emit("ve", lambda E, c: E.tensor_scalar_add(
        out=c.sb["rr"][:], in0=c.sb["rr"][:], scalar1=1e-5), inc="v")
    p.dep("se", t4)
    t5 = p.emit("se", lambda E, c: E.activation(
        out=c.sb["rr"][:], in_=c.sb["rr"][:], func=AF.Sqrt), inc="s")
    p.dep("ve", t5)
    t6 = p.emit("ve", lambda E, c: E.reciprocal(out=c.sb["rr"][:], in_=c.sb["rr"][:]), inc="v")
    tok_stats = p.tok("v")

    prev_tw = {}
    prev_th = {}
    prev_tm = {}
    prev_tn = {}
    for i in range(NTILE):
        par = i % 2
        bo = par * D
        ho = par * 64
        if par in prev_tn:
            p.dep("sy", prev_tn[par])
        tin = p.emit("sy", lambda E, c, i=i, bo=bo: E.dma_start(
            out=c.sb["xin"][:, bo:bo + D], in_=c.dram["xfull"][128 * i:128 * (i + 1), :]),
            inc="dy", inc_by=16)
        p.dep("ve", tin)
        p.dep("ve", tok_stats)
        if par in prev_tw:
            p.dep("ve", prev_tw[par])
            p.dep("ve", prev_th[par])
        tn = p.emit("ve", lambda E, c, i=i, bo=bo: E.tensor_scalar(
            out=c.sb["lnt"][:, bo:bo + D], in0=c.sb["xin"][:, bo:bo + D],
            scalar1=c.sb["mu"][:, i:i + 1], scalar2=c.sb["rr"][:, i:i + 1],
            op0=ALU.subtract, op1=ALU.mult), inc="v")
        prev_tn[par] = tn
        p.dep("ge", tn)
        tw = p.emit("ge", lambda E, c, i=i, bo=bo: E.dma_start(
            out=c.dram["ln0d"][128 * i:128 * (i + 1), :], in_=c.sb["lnt"][:, bo:bo + D]),
            inc="dg", inc_by=16)
        prev_tw[par] = tw
        if par in prev_tm:
            p.dep("ve", prev_tm[par])
        th = p.emit("ve", lambda E, c, i=i, bo=bo, ho=ho: E.tensor_reduce(
            out=c.sb["hsum"][:, ho:ho + 64],
            in_=c.sb["lnt"].ap()[:, bo:bo + D].rearrange("p (h k) -> p k h", h=16),
            axis=AX.X, op=ALU.add), inc="v")
        prev_th[par] = th
        l = i % 32
        p.dep("ve", th)
        if i < 32:
            tm = p.emit("ve", lambda E, c, l=l, ho=ho: E.tensor_copy(
                out=c.sb["mtm"][:, 64 * l:64 * l + 64], in_=c.sb["hsum"][:, ho:ho + 64]), inc="v")
        else:
            tm = p.emit("ve", lambda E, c, l=l, ho=ho: E.tensor_add(
                out=c.sb["mtm"][:, 64 * l:64 * l + 64],
                in0=c.sb["mtm"][:, 64 * l:64 * l + 64],
                in1=c.sb["hsum"][:, ho:ho + 64]), inc="v")
        prev_tm[par] = tm
    tok_m = p.tok("v")

    p.dep("ve", tok_m)
    tmb = p.emit("ve", lambda E, c: E.tensor_scalar_mul(
        out=c.sb["mtm_b"][:], in0=c.sb["mtm"][:], scalar1=1.0 / 64.0), inc="v")
    tok_mtmb = p.tok("v")

    tub = p.emit("ge", lambda E, c: E.memset(c.sb["ident"][:], 1.0), inc="g")
    p.dep("ge", tub)
    tid = p.emit("ge", lambda E, c: E.affine_select(
        out=c.sb["ident"][:], in_=c.sb["ident"][:], pattern=[[1, 128]],
        compare_op=ALU.is_equal, fill=0.0, base=0, channel_multiplier=-1), inc="g")
    tones = p.emit("ge", lambda E, c: E.memset(c.sb["onesc"][:], 1.0), inc="g")
    tzone = p.emit("ge", lambda E, c: E.memset(c.sb["zone"][:], 1.0), inc="g")

    p.dep("te", tid)
    p.dep("te", tok_m)
    for l in range(32):
        tt = p.emit("te", lambda E, c, l=l: E.transpose(
            c.ps["P"][0:64, 0:128], c.sb["mtm"][:, 64 * l:64 * l + 64],
            c.sb["ident"][:]), inc="t")
        p.dep("ve", tt)
        tv = p.emit("ve", lambda E, c, l=l: E.tensor_scalar_mul(
            out=c.sb["mfm_b"][:, 128 * l:128 * (l + 1)],
            in0=c.ps["P"][0:64, 0:128], scalar1=1.0 / 64.0), inc="v")
        p.dep("te", tv)
    tok_mfm = p.tok("v")

    for l in range(32):
        so = (l % 2) * 512
        tld = p.emit("sy", lambda E, c, l=l, so=so: E.dma_start(
            out=c.sb["iselt"][:, so:so + 512], in_=c.dram["isel"][128 * l:128 * (l + 1), :]),
            inc="dy", inc_by=16)
        p.dep("te", tld)
        p.dep("te", tok_mtmb)
        tm = p.emit("te", lambda E, c, l=l, so=so: E.matmul(
            c.ps["P"][0:64, 512:1024], c.sb["mtm_b"][:, 64 * l:64 * l + 64],
            c.sb["iselt"][:, so:so + 512], start=(l == 0), stop=(l == 31)), inc="t")
        if l >= 1:
            p.wait("sy", "t")
    tok_msl_mm = p.tok("t")
    p.dep("ve", tok_msl_mm)
    tev = p.emit("ve", lambda E, c: E.tensor_copy(
        out=c.sb["msl_b"][:], in_=c.ps["P"][0:64, 512:1024]), inc="v")
    tok_msl = p.tok("v")

    for j in range(32):
        so = (j % 2) * 512
        p.dep("te", tok_msl)
        p.dep("te", tok_mfm)
        bank = 1024 + (j % 2) * 512
        tmm = p.emit("te", lambda E, c, j=j, bank=bank: E.matmul(
            c.ps["P"][:, bank:bank + 512], c.sb["mfm_b"][:, 128 * j:128 * (j + 1)],
            c.sb["msl_b"][:], start=True, stop=True), inc="t")
        p.dep("ve", tmm)
        tcl = p.emit("ve", lambda E, c, j=j, so=so, bank=bank: E.tensor_scalar(
            out=c.sb["sf32"][:, so:so + 512], in0=c.ps["P"][:, bank:bank + 512],
            scalar1=-np.pi, scalar2=np.pi, op0=ALU.max, op1=ALU.min), inc="v")
        p.dep("te", tcl)
        p.dep("se", tcl)
        tsin = p.emit("se", lambda E, c, j=j, so=so: E.activation(
            out=c.sb["sf32"][:, so:so + 512], in_=c.sb["sf32"][:, so:so + 512],
            func=AF.Sin, scale=0.5), inc="s")
        p.dep("ve", tsin)
        tsq = p.emit("ve", lambda E, c, j=j, so=so: E.tensor_tensor(
            out=c.sb["sf32"][:, so:so + 512], in0=c.sb["sf32"][:, so:so + 512],
            in1=c.sb["sf32"][:, so:so + 512], op=ALU.mult), inc="v")
        p.dep("se", tsq)
        texp = p.emit("se", lambda E, c, j=j, so=so: E.activation(
            out=c.sb["strip"][:, 512 * j:512 * (j + 1)], in_=c.sb["sf32"][:, so:so + 512],
            func=AF.Exp), inc="s")
        p.dep("ve", texp)
    tok_strips = p.tok("s")

    p.dep("te", tok_strips)
    p.dep("te", tones)
    for j in range(32):
        p.emit("te", lambda E, c, j=j: E.matmul(
            c.ps["P"][0:1, 0:512], c.sb["onesc"][:],
            c.sb["strip"][:, 512 * j:512 * (j + 1)], start=(j == 0), stop=(j == 31)), inc="t")
    tz = p.tok("t")
    p.dep("ve", tz)
    tzr = p.emit("ve", lambda E, c: E.tensor_copy(out=c.sb["zrow"][:], in_=c.ps["P"][0:1, 0:512]), inc="v")
    p.dep("ve", tzr)
    tzc = p.emit("ve", lambda E, c: E.reciprocal(out=c.sb["zrow"][:], in_=c.sb["zrow"][:]), inc="v")
    p.dep("ve", tzc)
    tzb = p.emit("ve", lambda E, c: E.tensor_copy(
        out=c.sb["lnj"][0:1, 0:512], in_=c.sb["zrow"][:]), inc="v")
    p.dep("te", tzb)
    p.dep("te", tzone)
    for isub in range(4):
        tkm = p.emit("te", lambda E, c, isub=isub: E.matmul(
            c.ps["P"][:, 2048 + 128 * isub:2048 + 128 * isub + 1],
            c.sb["lnj"][0:1, 128 * isub:128 * (isub + 1)],
            c.sb["zone"][:], start=True, stop=True), inc="t")
    tzk = p.tok("t")
    p.dep("ve", tzk)
    for isub in range(4):
        p.emit("ve", lambda E, c, isub=isub: E.tensor_copy(
            out=c.sb["zcol"][:, isub:isub + 1],
            in_=c.ps["P"][:, 2048 + 128 * isub:2048 + 128 * isub + 1]), inc="v")
    tok_zcol = p.tok("v")
    p.dep("te", tok_zcol)

    p.wait("sy", "t")
    p.wait("sy", "dg")
    prev_lnj = {}
    prev_ev = {}
    prev_st = {}
    prev_xev = {}
    for b in range(B):
        for j in range(32):
            par = j % 2
            so = par * D
            row = b * S + 128 * j
            if par in prev_lnj:
                p.dep("sy", prev_lnj[par])
            tld = p.emit("sy", lambda E, c, row=row, so=so: E.dma_start(
                out=c.sb["lnj"][:, so:so + D], in_=c.dram["ln0d"][row:row + 128, :]),
                inc="dy", inc_by=16)
            p.dep("te", tld)
            for isub in range(4):
                for h in range(2):
                    tmm = p.emit("te", lambda E, c, j=j, so=so, isub=isub, h=h: E.matmul(
                        c.ps["P"][:, 1024 * isub + 512 * h:1024 * isub + 512 * h + 512],
                        c.sb["strip"][:, 512 * j + 128 * isub:512 * j + 128 * isub + 128],
                        c.sb["lnj"][:, so + 512 * h:so + 512 * h + 512],
                        start=(j == 0), stop=(j == 31)), inc="t")
            prev_lnj[par] = tmm
        tok_mm = p.tok("t")
        for isub in range(4):
            par = isub % 2
            so = par * D
            row = b * 512 + 128 * isub
            slot = par
            if slot in prev_xev:
                p.dep("sy", prev_xev[slot])
            tldx = p.emit("sy", lambda E, c, row=row, so=so: E.dma_start(
                out=c.sb["xres"][:, so:so + D], in_=c.dram["xsl"][row:row + 128, :]),
                inc="dy", inc_by=16)
            p.dep("ve", tok_mm)
            p.dep("ve", tldx)
            p.dep("ve", tok_zcol)
            if slot in prev_st:
                p.dep("ve", prev_st[slot])
            for h in range(2):
                tev = p.emit("ve", lambda E, c, isub=isub, so=so, h=h: E.scalar_tensor_tensor(
                    out=c.sb["outt"][:, so + 512 * h:so + 512 * h + 512],
                    in0=c.ps["P"][:, 1024 * isub + 512 * h:1024 * isub + 512 * h + 512],
                    scalar=c.sb["zcol"][:, isub:isub + 1],
                    in1=c.sb["xres"][:, so + 512 * h:so + 512 * h + 512],
                    op0=ALU.mult, op1=ALU.add), inc="v")
            prev_ev[slot] = tev
            prev_xev[slot] = tev
            p.dep("ge", tev)
            tst = p.emit("ge", lambda E, c, row=row, so=so: E.dma_start(
                out=c.dram["x1sl"][row:row + 128, :], in_=c.sb["outt"][:, so:so + D]),
                inc="dg", inc_by=16)
            prev_st[slot] = tst
        p.wait("te", "v")

    p.wait("ge", "dg")
    p.wait("sy", "dg")

    nc = p.build(nc, sbufs, psums, c)
    return nc


def emit_quant(p, c, eng, src_ap_fn, nb, qint_sl, scl_sl, rscl_sl, out_bf_fn, dep_tok=None):
    if dep_tok is not None:
        p.dep(eng, dep_tok)
    t = p.emit(eng, lambda E, c: E.tensor_reduce(
        out=scl_sl(c), in_=src_ap_fn(c).rearrange("p (b k) -> p b k", k=32),
        axis=AX.X, op=ALU.max, apply_absolute_value=True), inc="v")
    p.dep(eng, t)
    t = p.emit(eng, lambda E, c: E.tensor_scalar_max(
        out=scl_sl(c), in0=scl_sl(c), scalar1=2.0 ** -126), inc="v")
    p.dep(eng, t)
    t = p.emit(eng, lambda E, c: E.tensor_single_scalar(
        out=scl_sl(c).bitcast(I32), in_=scl_sl(c).bitcast(I32),
        scalar=0x7F800000, op=ALU.bitwise_and), inc="v")
    p.dep(eng, t)
    t = p.emit(eng, lambda E, c: E.tensor_scalar(
        out=rscl_sl(c).bitcast(I32), in0=scl_sl(c).bitcast(I32),
        scalar1=254 << 23, scalar2=-1, op0=ALU.subtract, op1=ALU.mult), inc="v")
    p.dep(eng, t)
    t = p.emit(eng, lambda E, c: E.tensor_scalar_mul(
        out=rscl_sl(c), in0=rscl_sl(c), scalar1=8.0), inc="v")
    p.dep(eng, t)
    t = p.emit(eng, lambda E, c: E.tensor_scalar_mul(
        out=scl_sl(c), in0=scl_sl(c), scalar1=0.125), inc="v")
    p.dep(eng, t)
    t = p.emit(eng, lambda E, c, nb=nb: E.tensor_tensor(
        out=qint_sl(c), in0=src_ap_fn(c).rearrange("p (b k) -> p b k", k=32),
        in1=rscl_sl(c).unsqueeze(2).broadcast_to([128, nb, 32]), op=ALU.mult), inc="v")
    p.dep(eng, t)
    t = p.emit(eng, lambda E, c, nb=nb: E.tensor_tensor(
        out=out_bf_fn(c), in0=qint_sl(c),
        in1=scl_sl(c).unsqueeze(2).broadcast_to([128, nb, 32]), op=ALU.mult), inc="v")
    return t


def build_A2():
    nc = bass.Bass("TRN2", target_bir_lowering=False, debug=False)
    p = Prog()
    c = Ctx()
    TROWS = 2048
    TT = TROWS // 128

    dram = {}
    dram["x1r"] = nc.dram_tensor("x1r", [TROWS, D], F32, kind="ExternalInput").ap()
    dram["wfc"] = nc.dram_tensor("wfc", [4096, D], F32, kind="ExternalInput").ap()
    dram["bfc"] = nc.dram_tensor("bfc", [1, 4096], F32, kind="ExternalInput").ap()
    dram["wproj"] = nc.dram_tensor("wproj", [D, 4096], F32, kind="ExternalInput").ap()
    dram["bproj"] = nc.dram_tensor("bproj", [1, D], F32, kind="ExternalInput").ap()
    dram["x2r"] = nc.dram_tensor("x2r", [TROWS, D], F32, kind="ExternalOutput").ap()
    dram["xqd"] = nc.dram_tensor("xqd", [TROWS, D], BF16).ap()
    dram["wfcqd"] = nc.dram_tensor("wfcqd", [4096, D], BF16).ap()
    dram["wprojqd"] = nc.dram_tensor("wprojqd", [D, 4096], BF16).ap()
    dram["hqd"] = nc.dram_tensor("hqd", [TROWS, 4096], BF16).ap()
    c.dram = dram

    sbufs = {
        "xin": ((128, D), F32),
        "stat": ((128, 2 * TT), F32),
        "mu": ((128, TT), F32),
        "rr": ((128, TT), F32),
        "ln1": ((128, D), F32),
        "qint": ((128, 4096), mybir.dt.int16),
        "scl": ((128, 128), F32),
        "rscl": ((128, 128), F32),
        "qbf": ((128, 4096), BF16),
        "onesr": ((1, 128), F32),
        "brow": ((1, 4096), F32),
        "bbc": ((128, 4096), F32),
        "bpbc": ((128, D), F32),
        "hb": ((128, 4096), F32),
        "outt": ((128, 2 * D), F32),
        "xres": ((128, 2 * D), F32),
        "arena": ((128, 49152), BF16),
    }
    psums = {"P": ((128, 4096), F32)}

    for s_ in ("dy", "dg", "t", "v", "s", "g"):
        p.sem(s_)

    def win(c):
        return c.sb["arena"].ap()[:, 0:8192].bitcast(F32)

    def xqfm(c, k):
        return c.sb["arena"].ap()[:, TROWS * k:TROWS * (k + 1)]

    def wfcfm(c, k):
        return c.sb["arena"].ap()[:, 16384 + 4096 * k:16384 + 4096 * (k + 1)]

    def wpfm(c, k):
        return c.sb["arena"].ap()[:, 1024 * k:1024 * (k + 1)]

    def hqfm(c, k):
        return c.sb["arena"].ap()[:, 32768 + 512 * k:32768 + 512 * (k + 1)]

    t_on = p.emit("ge", lambda E, c: E.memset(c.sb["onesr"][:], 1.0), inc="g")
    tb1 = p.emit("sy", lambda E, c: E.dma_start(out=c.sb["brow"][:], in_=c.dram["bfc"]), inc="dy", inc_by=16)
    p.dep("te", t_on)
    p.dep("te", tb1)
    p.wait("te", "g")
    for n in range(8):
        tm = p.emit("te", lambda E, c, n=n: E.matmul(
            c.ps["P"][:, 512 * (n % 4):512 * (n % 4) + 512], c.sb["onesr"][:],
            c.sb["brow"][:, 512 * n:512 * (n + 1)], start=True, stop=True), inc="t")
        p.dep("ve", tm)
        tv = p.emit("ve", lambda E, c, n=n: E.tensor_copy(
            out=c.sb["bbc"][:, 512 * n:512 * (n + 1)],
            in_=c.ps["P"][:, 512 * (n % 4):512 * (n % 4) + 512]), inc="v")
        p.dep("te", tv)
    p.wait("sy", "t")
    tb2 = p.emit("sy", lambda E, c: E.dma_start(out=c.sb["brow"][:, 0:D], in_=c.dram["bproj"]), inc="dy", inc_by=16)
    p.dep("te", tb2)
    p.wait("te", "v")
    for n in range(2):
        tm = p.emit("te", lambda E, c, n=n: E.matmul(
            c.ps["P"][:, 512 * n:512 * n + 512], c.sb["onesr"][:],
            c.sb["brow"][:, 512 * n:512 * (n + 1)], start=True, stop=True), inc="t")
        p.dep("ve", tm)
        tv = p.emit("ve", lambda E, c, n=n: E.tensor_copy(
            out=c.sb["bpbc"][:, 512 * n:512 * (n + 1)],
            in_=c.ps["P"][:, 512 * n:512 * n + 512]), inc="v")
        p.dep("te", tv)
    tok_bias = p.tok("v")

    for i in range(TT):
        p.wait("sy", "v")
        p.wait("sy", "s")
        tin = p.emit("sy", lambda E, c, i=i: E.dma_start(
            out=c.sb["xin"][:], in_=c.dram["x1r"][128 * i:128 * (i + 1), :]),
            inc="dy", inc_by=16)
        p.dep("ve", tin)
        p.emit("ve", lambda E, c, i=i: E.reduce_sum(
            out=c.sb["stat"][:, i:i + 1], in_=c.sb["xin"][:], axis=AX.X), inc="v")
        p.dep("se", tin)
        p.emit("se", lambda E, c, i=i: E.activation(
            out=c.sb["ln1"][:], in_=c.sb["xin"][:], func=AF.Square,
            accum_out=c.sb["stat"][:, TT + i:TT + i + 1]), inc="s")
        p.wait("se", "s")
    p.wait("ve", "v")
    p.wait("ve", "s")
    t1 = p.emit("ve", lambda E, c: E.tensor_scalar_mul(
        out=c.sb["mu"][:], in0=c.sb["stat"][:, 0:TT], scalar1=1.0 / D), inc="v")
    p.dep("ve", t1)
    t2 = p.emit("ve", lambda E, c: E.tensor_tensor(
        out=c.sb["rr"][:], in0=c.sb["mu"][:], in1=c.sb["mu"][:], op=ALU.mult), inc="v")
    p.dep("ve", t2)
    t3 = p.emit("ve", lambda E, c: E.scalar_tensor_tensor(
        out=c.sb["rr"][:], in0=c.sb["stat"][:, TT:2 * TT], scalar=1.0 / D,
        in1=c.sb["rr"][:], op0=ALU.mult, op1=ALU.subtract), inc="v")
    p.dep("ve", t3)
    t4 = p.emit("ve", lambda E, c: E.tensor_scalar_add(
        out=c.sb["rr"][:], in0=c.sb["rr"][:], scalar1=1e-5), inc="v")
    p.dep("se", t4)
    t5 = p.emit("se", lambda E, c: E.activation(
        out=c.sb["rr"][:], in_=c.sb["rr"][:], func=AF.Sqrt), inc="s")
    p.dep("ve", t5)
    t6 = p.emit("ve", lambda E, c: E.reciprocal(out=c.sb["rr"][:], in_=c.sb["rr"][:]), inc="v")
    tok_stats = p.tok("v")

    def quant_tile(src_fn, nb, dst_dram_fn, dep_tok):
        tq = emit_quant(
            p, c, "ve", src_fn, nb,
            lambda c, nb=nb: c.sb["qint"].ap()[:, 0:nb * 32].rearrange("p (b k) -> p b k", k=32),
            lambda c, nb=nb: c.sb["scl"].ap()[:, 0:nb],
            lambda c, nb=nb: c.sb["rscl"].ap()[:, 0:nb],
            lambda c, nb=nb: c.sb["qbf"].ap()[:, 0:nb * 32].rearrange("p (b k) -> p b k", k=32),
            dep_tok=dep_tok)
        p.dep("ge", tq)
        tw = p.emit("ge", lambda E, c: dst_dram_fn(E, c), inc="dg", inc_by=16)
        p.wait("ve", "dg")
        return tw

    for i in range(TT):
        p.wait("sy", "v")
        tin = p.emit("sy", lambda E, c, i=i: E.dma_start(
            out=c.sb["xin"][:], in_=c.dram["x1r"][128 * i:128 * (i + 1), :]),
            inc="dy", inc_by=16)
        p.dep("ve", tin)
        p.dep("ve", tok_stats)
        tn = p.emit("ve", lambda E, c, i=i: E.tensor_scalar(
            out=c.sb["ln1"][:], in0=c.sb["xin"][:],
            scalar1=c.sb["mu"][:, i:i + 1], scalar2=c.sb["rr"][:, i:i + 1],
            op0=ALU.subtract, op1=ALU.mult), inc="v")
        quant_tile(lambda c: c.sb["ln1"].ap()[:], 32,
                   lambda E, c, i=i: E.dma_start(
                       out=c.dram["xqd"][128 * i:128 * (i + 1), :], in_=c.sb["qbf"][:, 0:D]),
                   tn)

    for i in range(32):
        p.wait("sy", "v")
        tin = p.emit("sy", lambda E, c, i=i: E.dma_start(
            out=win(c)[:, 0:D], in_=c.dram["wfc"][128 * i:128 * (i + 1), :]),
            inc="dy", inc_by=16)
        p.dep("ve", tin)
        quant_tile(lambda c: win(c)[:, 0:D], 32,
                   lambda E, c, i=i: E.dma_start(
                       out=c.dram["wfcqd"][128 * i:128 * (i + 1), :], in_=c.sb["qbf"][:, 0:D]),
                   tin)
    for i in range(8):
        p.wait("sy", "v")
        tin = p.emit("sy", lambda E, c, i=i: E.dma_start(
            out=win(c)[:, 0:4096], in_=c.dram["wproj"][128 * i:128 * (i + 1), :]),
            inc="dy", inc_by=16)
        p.dep("ve", tin)
        quant_tile(lambda c: win(c)[:, 0:4096], 128,
                   lambda E, c, i=i: E.dma_start(
                       out=c.dram["wprojqd"][128 * i:128 * (i + 1), :], in_=c.sb["qbf"][:, 0:4096]),
                   tin)

    p.wait("sy", "dg")
    p.wait("sy", "v")
    for k in range(8):
        p.emit("sy", lambda E, c, k=k: E.dma_start_transpose(
            out=xqfm(c, k), in_=c.dram["xqd"][:, 128 * k:128 * (k + 1)]), inc="dy", inc_by=16)
        p.emit("sy", lambda E, c, k=k: E.dma_start_transpose(
            out=wfcfm(c, k), in_=c.dram["wfcqd"][:, 128 * k:128 * (k + 1)]), inc="dy", inc_by=16)
    tok_fm = p.tok("dy")

    p.dep("te", tok_fm)
    p.dep("te", tok_bias)
    for t in range(TT):
        for K in range(8):
            for n in range(8):
                tm = p.emit("te", lambda E, c, t=t, K=K, n=n: E.matmul(
                    c.ps["P"][:, 512 * n:512 * (n + 1)],
                    xqfm(c, K)[:, 128 * t:128 * (t + 1)],
                    wfcfm(c, K)[:, 512 * n:512 * (n + 1)],
                    start=(K == 0), stop=(K == 7)), inc="t")
        tok_mm1 = p.tok("t")
        p.dep("ve", tok_mm1)
        for n in range(8):
            tev = p.emit("ve", lambda E, c, n=n: E.scalar_tensor_tensor(
                out=c.sb["hb"][:, 512 * n:512 * (n + 1)],
                in0=c.ps["P"][:, 512 * n:512 * (n + 1)], scalar=1.0,
                in1=c.sb["bbc"][:, 512 * n:512 * (n + 1)], op0=ALU.mult, op1=ALU.add), inc="v")
        tok_ev = p.tok("v")
        p.dep("te", tok_ev)
        p.dep("se", tok_ev)
        tg = p.emit("se", lambda E, c: E.activation(
            out=c.sb["hb"][:], in_=c.sb["hb"][:], func=AF.Gelu), inc="s")
        quant_tile(lambda c: c.sb["hb"].ap()[:], 128,
                   lambda E, c, t=t: E.dma_start(
                       out=c.dram["hqd"][128 * t:128 * (t + 1), :], in_=c.sb["qbf"][:, 0:4096]),
                   tg)
        p.wait("ve", "s")

    p.wait("sy", "dg")
    p.wait("sy", "t")
    for k in range(32):
        p.emit("sy", lambda E, c, k=k: E.dma_start_transpose(
            out=wpfm(c, k), in_=c.dram["wprojqd"][:, 128 * k:128 * (k + 1)]), inc="dy", inc_by=16)
    for quarter in range(4):
        if quarter >= 1:
            p.wait("sy", "t")
        for k in range(32):
            p.emit("sy", lambda E, c, k=k, quarter=quarter: E.dma_start_transpose(
                out=hqfm(c, k),
                in_=c.dram["hqd"][512 * quarter:512 * (quarter + 1), 128 * k:128 * (k + 1)]),
                inc="dy", inc_by=16)
        tok_hq = p.tok("dy")
        p.dep("te", tok_hq)
        for t4 in range(4):
            t = quarter * 4 + t4
            bank = (t4 % 4) * 1024
            for K in range(32):
                for n in range(2):
                    tm = p.emit("te", lambda E, c, t4=t4, K=K, n=n, bank=bank: E.matmul(
                        c.ps["P"][:, bank + 512 * n:bank + 512 * (n + 1)],
                        hqfm(c, K)[:, 128 * t4:128 * (t4 + 1)],
                        wpfm(c, K)[:, 512 * n:512 * (n + 1)],
                        start=(K == 0), stop=(K == 31)), inc="t")
            tok_mm2 = p.tok("t")
            par = t % 2
            so = par * D
            p.wait("sy", "v")
            tldx = p.emit("sy", lambda E, c, t=t, so=so: E.dma_start(
                out=c.sb["xres"][:, so:so + D], in_=c.dram["x1r"][128 * t:128 * (t + 1), :]),
                inc="dy", inc_by=16)
            p.dep("ve", tok_mm2)
            p.dep("ve", tldx)
            p.dep("ve", tok_bias)
            p.wait("ve", "dg")
            tev = p.emit("ve", lambda E, c, bank=bank, so=so: E.scalar_tensor_tensor(
                out=c.sb["outt"][:, so:so + D], in0=c.ps["P"][:, bank:bank + 1024],
                scalar=1.0, in1=c.sb["bpbc"][:], op0=ALU.mult, op1=ALU.add), inc="v")
            p.dep("ve", tev)
            tev2 = p.emit("ve", lambda E, c, so=so: E.tensor_add(
                out=c.sb["outt"][:, so:so + D], in0=c.sb["outt"][:, so:so + D],
                in1=c.sb["xres"][:, so:so + D]), inc="v")
            p.dep("ge", tev2)
            tst = p.emit("ge", lambda E, c, t=t, so=so: E.dma_start(
                out=c.dram["x2r"][128 * t:128 * (t + 1), :], in_=c.sb["outt"][:, so:so + D]),
                inc="dg", inc_by=16)
            p.dep("te", tev)

    p.wait("ge", "dg")
    p.wait("sy", "dg")

    nc = p.build(nc, sbufs, psums, c)
    return nc


def build_B():
    nc = bass.Bass("TRN2", target_bir_lowering=False, debug=False)
    p = Prog()
    c = Ctx()
    NCH = 32

    dram = {}
    dram["x2b"] = nc.dram_tensor("x2b", [S, D], F32, kind="ExternalInput").ap()
    for w in ("wqT", "wkT", "wvT", "wgT"):
        dram[w] = nc.dram_tensor(w, [D, 512], F32, kind="ExternalInput").ap()
    dram["woTs"] = nc.dram_tensor("woTs", [512, D], F32, kind="ExternalInput").ap()
    dram["gnw"] = nc.dram_tensor("gnw", [1, 512], F32, kind="ExternalInput").ap()
    dram["gnb"] = nc.dram_tensor("gnb", [1, 512], F32, kind="ExternalInput").ap()
    dram["cosT"] = nc.dram_tensor("cosT", [S, 64], F32, kind="ExternalInput").ap()
    dram["sinT"] = nc.dram_tensor("sinT", [S, 64], F32, kind="ExternalInput").ap()
    dram["grq"] = nc.dram_tensor("grq", [128, 8], F32, kind="ExternalInput").ap()
    dram["grkn"] = nc.dram_tensor("grkn", [128, 8], F32, kind="ExternalInput").ap()
    dram["grks"] = nc.dram_tensor("grks", [128, 8], F32, kind="ExternalInput").ap()
    dram["g128"] = nc.dram_tensor("g128", [64, 8], F32, kind="ExternalInput").ap()
    dram["tri01"] = nc.dram_tensor("tri01", [128, 128], F32, kind="ExternalInput").ap()
    dram["pout"] = nc.dram_tensor("pout", [S, D], F32, kind="ExternalOutput").ap()
    dram["ln2d"] = nc.dram_tensor("ln2d", [S, D], BF16).ap()
    c.dram = dram

    sbufs = {
        "xin": ((128, D), F32),
        "stat": ((128, 2 * NCH), F32),
        "mu": ((128, NCH), F32),
        "rr": ((128, NCH), F32),
        "dump": ((128, D), F32),
        "lnt": ((128, D), BF16),
        "ln2fm": ((128, 8 * S), BF16),
        "wq": ((128, 8 * 512), BF16),
        "wk": ((128, 8 * 512), BF16),
        "wv": ((128, 8 * 512), BF16),
        "wg": ((128, 8 * 512), BF16),
        "wo": ((128, 4 * D), BF16),
        "cosb": ((128, 32 * 64), F32),
        "sinb": ((128, 32 * 64), F32),
        "sgrq": ((128, 8), F32),
        "sgrkn": ((128, 8), F32),
        "sgrks": ((128, 8), F32),
        "sg128": ((64, 8), F32),
        "tri": ((128, 128), F32),
        "gnwb": ((128, 512), F32),
        "gnbb": ((128, 512), F32),
        "onesr": ((1, 512), F32),
        "grow": ((1, 512), F32),
        "identb": ((128, 128), BF16),
        "qt": ((128, 512), F32),
        "kt": ((128, 512), F32),
        "rtmp": ((128, 512), F32),
        "qq": ((128, 512), BF16),
        "kn": ((128, 512), BF16),
        "ks": ((128, 512), BF16),
        "vb": ((128, 512), BF16),
        "gb": ((128, 512), BF16),
        "qqf": ((128, 512), BF16),
        "knf": ((128, 512), BF16),
        "qqlo": ((64, 512), BF16),
        "sab": ((128, 2 * 128), BF16),
        "stf": ((64, 512), F32),
        "stb": ((64, 512), BF16),
        "ret": ((128, 512), F32),
        "gy": ((128, 512), BF16),
        "gyf": ((128, 512), BF16),
        "gnm": ((128, 8), F32),
        "gnv": ((128, 8), F32),
        "outt": ((128, 2 * D), F32),
    }
    psums = {"P": ((128, 4096), F32)}

    for s_ in ("dy", "dg", "t", "v", "s", "g"):
        p.sem(s_)

    for name, dn in (("sgrq", "grq"), ("sgrkn", "grkn"), ("sgrks", "grks"), ("sg128", "g128")):
        p.emit("sy", lambda E, c, name=name, dn=dn: E.dma_start(out=c.sb[name][:], in_=c.dram[dn]), inc="dy", inc_by=16)
    p.emit("sy", lambda E, c: E.dma_start(out=c.sb["tri"][:], in_=c.dram["tri01"]), inc="dy", inc_by=16)
    p.emit("sy", lambda E, c: E.dma_start(
        out=c.sb["cosb"].ap().rearrange("p (n k) -> p n k", k=64),
        in_=c.dram["cosT"].rearrange("(n p) k -> p n k", p=128)), inc="dy", inc_by=16)
    p.emit("sy", lambda E, c: E.dma_start(
        out=c.sb["sinb"].ap().rearrange("p (n k) -> p n k", k=64),
        in_=c.dram["sinT"].rearrange("(n p) k -> p n k", p=128)), inc="dy", inc_by=16)
    for w, dst in (("wqT", "wq"), ("wkT", "wk"), ("wvT", "wv"), ("wgT", "wg")):
        for k in range(8):
            p.emit("ge", lambda E, c, w=w, dst=dst, k=k: E.dma_start(
                out=c.sb[dst][:, 512 * k:512 * (k + 1)],
                in_=c.dram[w][128 * k:128 * (k + 1), :]), inc="dg", inc_by=16)
    for k in range(4):
        p.emit("ge", lambda E, c, k=k: E.dma_start(
            out=c.sb["wo"][:, D * k:D * (k + 1)],
            in_=c.dram["woTs"][128 * k:128 * (k + 1), :]), inc="dg", inc_by=16)
    t_on = p.emit("ge", lambda E, c: E.memset(c.sb["onesr"][:, 0:128], 1.0), inc="g")
    tgw = p.emit("sy", lambda E, c: E.dma_start(out=c.sb["grow"][:], in_=c.dram["gnw"]), inc="dy", inc_by=16)
    p.dep("te", t_on)
    p.wait("te", "g")
    p.dep("te", tgw)
    tm1 = p.emit("te", lambda E, c: E.matmul(
        c.ps["P"][:, 0:512], c.sb["onesr"][0:1, 0:128], c.sb["grow"][0:1, 0:512],
        start=True, stop=True), inc="t")
    p.dep("ve", tm1)
    tv1 = p.emit("ve", lambda E, c: E.tensor_copy(out=c.sb["gnwb"][:], in_=c.ps["P"][:, 0:512]), inc="v")
    p.dep("sy", tv1)
    p.dep("te", tv1)
    tgb = p.emit("sy", lambda E, c: E.dma_start(out=c.sb["grow"][:], in_=c.dram["gnb"]), inc="dy", inc_by=16)
    p.dep("te", tgb)
    tm2 = p.emit("te", lambda E, c: E.matmul(
        c.ps["P"][:, 0:512], c.sb["onesr"][0:1, 0:128], c.sb["grow"][0:1, 0:512],
        start=True, stop=True), inc="t")
    p.dep("ve", tm2)
    tv2 = p.emit("ve", lambda E, c: E.tensor_copy(out=c.sb["gnbb"][:], in_=c.ps["P"][:, 0:512]), inc="v")
    p.dep("te", tv2)
    p.emit("ge", lambda E, c: E.memset(c.sb["stf"][:], 0.0), inc="g")
    p.emit("ge", lambda E, c: E.memset(c.sb["stb"][:], 0.0), inc="g")
    tunb = p.emit("ge", lambda E, c: E.memset(c.sb["identb"][:], 1.0), inc="g")
    p.dep("ge", tunb)
    p.emit("ge", lambda E, c: E.affine_select(
        out=c.sb["identb"][:], in_=c.sb["identb"][:], pattern=[[1, 128]],
        compare_op=ALU.is_equal, fill=0.0, base=0, channel_multiplier=-1), inc="g")

    for i in range(NCH):
        p.wait("sy", "v")
        p.wait("sy", "s")
        tin = p.emit("sy", lambda E, c, i=i: E.dma_start(
            out=c.sb["xin"][:], in_=c.dram["x2b"][128 * i:128 * (i + 1), :]), inc="dy", inc_by=16)
        p.dep("ve", tin)
        p.emit("ve", lambda E, c, i=i: E.reduce_sum(
            out=c.sb["stat"][:, i:i + 1], in_=c.sb["xin"][:], axis=AX.X), inc="v")
        p.dep("se", tin)
        p.emit("se", lambda E, c, i=i: E.activation(
            out=c.sb["dump"][:], in_=c.sb["xin"][:], func=AF.Square,
            accum_out=c.sb["stat"][:, NCH + i:NCH + i + 1]), inc="s")
        p.wait("se", "s")
    p.wait("ve", "v")
    p.wait("ve", "s")
    t1 = p.emit("ve", lambda E, c: E.tensor_scalar_mul(
        out=c.sb["mu"][:], in0=c.sb["stat"][:, 0:NCH], scalar1=1.0 / D), inc="v")
    p.dep("ve", t1)
    t2 = p.emit("ve", lambda E, c: E.tensor_tensor(
        out=c.sb["rr"][:], in0=c.sb["mu"][:], in1=c.sb["mu"][:], op=ALU.mult), inc="v")
    p.dep("ve", t2)
    t3 = p.emit("ve", lambda E, c: E.scalar_tensor_tensor(
        out=c.sb["rr"][:], in0=c.sb["stat"][:, NCH:2 * NCH], scalar=1.0 / D,
        in1=c.sb["rr"][:], op0=ALU.mult, op1=ALU.subtract), inc="v")
    p.dep("ve", t3)
    t4 = p.emit("ve", lambda E, c: E.tensor_scalar_add(
        out=c.sb["rr"][:], in0=c.sb["rr"][:], scalar1=1e-5), inc="v")
    p.dep("se", t4)
    t5 = p.emit("se", lambda E, c: E.activation(
        out=c.sb["rr"][:], in_=c.sb["rr"][:], func=AF.Sqrt), inc="s")
    p.dep("ve", t5)
    t6 = p.emit("ve", lambda E, c: E.reciprocal(out=c.sb["rr"][:], in_=c.sb["rr"][:]), inc="v")
    tok_stats = p.tok("v")
    prev_w = None
    for i in range(NCH):
        p.wait("sy", "v")
        tin = p.emit("sy", lambda E, c, i=i: E.dma_start(
            out=c.sb["xin"][:], in_=c.dram["x2b"][128 * i:128 * (i + 1), :]), inc="dy", inc_by=16)
        p.dep("ve", tin)
        p.dep("ve", tok_stats)
        if prev_w is not None:
            p.dep("ve", prev_w)
        tn = p.emit("ve", lambda E, c, i=i: E.tensor_scalar(
            out=c.sb["lnt"][:], in0=c.sb["xin"][:],
            scalar1=c.sb["mu"][:, i:i + 1], scalar2=c.sb["rr"][:, i:i + 1],
            op0=ALU.subtract, op1=ALU.mult), inc="v")
        p.dep("ge", tn)
        prev_w = p.emit("ge", lambda E, c, i=i: E.dma_start(
            out=c.dram["ln2d"][128 * i:128 * (i + 1), :], in_=c.sb["lnt"][:]), inc="dg", inc_by=16)
    p.wait("sy", "dg")
    for k in range(8):
        p.emit("sy", lambda E, c, k=k: E.dma_start_transpose(
            out=c.sb["ln2fm"][:, S * k:S * (k + 1)],
            in_=c.dram["ln2d"][:, 128 * k:128 * (k + 1)]), inc="dy", inc_by=16)
    tok_lnfm = p.tok("dy")

    p.dep("te", tok_lnfm)
    p.wait("te", "dg")
    p.wait("te", "g")
    prev_pout_ev = None
    for ci in range(NCH):
        for iw, wname in enumerate(("wq", "wk", "wv", "wg")):
            for K in range(8):
                p.emit("te", lambda E, c, ci=ci, iw=iw, wname=wname, K=K: E.matmul(
                    c.ps["P"][:, 512 * iw:512 * (iw + 1)],
                    c.sb["ln2fm"][:, S * K + 128 * ci:S * K + 128 * (ci + 1)],
                    c.sb[wname][:, 512 * K:512 * (K + 1)],
                    start=(K == 0), stop=(K == 7)), inc="t")
        tok_proj = p.tok("t")
        p.dep("ve", tok_proj)
        p.dep("se", tok_proj)
        tq = p.emit("ve", lambda E, c: E.tensor_copy(out=c.sb["qt"][:], in_=c.ps["P"][:, 0:512]), inc="v")
        tk = p.emit("ve", lambda E, c: E.tensor_copy(out=c.sb["kt"][:], in_=c.ps["P"][:, 512:1024]), inc="v")
        tv = p.emit("ve", lambda E, c: E.tensor_copy(out=c.sb["vb"][:], in_=c.ps["P"][:, 1024:1536]), inc="v")
        tg = p.emit("se", lambda E, c: E.activation(
            out=c.sb["gb"][:], in_=c.ps["P"][:, 1536:2048], func=AF.Silu), inc="s")
        p.dep("te", tv)
        p.dep("te", tg)

        def rope(src, ci, dep0):
            co = 64 * ci
            r3 = lambda ap: ap.rearrange("p (h k) -> p h k", k=64)
            p.dep("ve", dep0)
            ta = p.emit("ve", lambda E, c, src=src, co=co, r3=r3: E.tensor_tensor(
                out=r3(c.sb["rtmp"].ap()), in0=r3(c.sb[src].ap()),
                in1=c.sb["cosb"].ap()[:, co:co + 64].unsqueeze(1).broadcast_to([128, 8, 64]),
                op=ALU.mult), inc="v")
            tb = p.emit("ve", lambda E, c, src=src, co=co, r3=r3: E.tensor_tensor(
                out=r3(c.sb["dump"].ap()[:, 0:512])[:, :, 32:64],
                in0=r3(c.sb[src].ap())[:, :, 0:32],
                in1=c.sb["sinb"].ap()[:, co + 32:co + 64].unsqueeze(1).broadcast_to([128, 8, 32]),
                op=ALU.mult), inc="v")
            p.dep("ve", tb)
            tc = p.emit("ve", lambda E, c, src=src, co=co, r3=r3: E.tensor_tensor(
                out=r3(c.sb[src].ap())[:, :, 0:32],
                in0=r3(c.sb[src].ap())[:, :, 32:64],
                in1=c.sb["sinb"].ap()[:, co:co + 32].unsqueeze(1).broadcast_to([128, 8, 32]),
                op=ALU.mult), inc="v")
            p.dep("ve", ta)
            p.dep("ve", tc)
            td = p.emit("ve", lambda E, c, src=src, r3=r3: E.tensor_tensor(
                out=r3(c.sb[src].ap())[:, :, 0:32],
                in0=r3(c.sb["rtmp"].ap())[:, :, 0:32],
                in1=r3(c.sb[src].ap())[:, :, 0:32], op=ALU.subtract), inc="v")
            p.dep("ve", td)
            te_ = p.emit("ve", lambda E, c, src=src, r3=r3: E.tensor_tensor(
                out=r3(c.sb[src].ap())[:, :, 32:64],
                in0=r3(c.sb["rtmp"].ap())[:, :, 32:64],
                in1=r3(c.sb["dump"].ap()[:, 0:512])[:, :, 32:64], op=ALU.add), inc="v")
            return te_

        rope("qt", ci, tq)
        rope("kt", ci, tk)

        tqq = p.emit("ve", lambda E, c: E.tensor_tensor(
            out=c.sb["qq"].ap().rearrange("p (h k) -> p h k", k=64),
            in0=c.sb["qt"].ap().rearrange("p (h k) -> p h k", k=64),
            in1=c.sb["sgrq"].ap().unsqueeze(2).broadcast_to([128, 8, 64]), op=ALU.mult), inc="v")
        tkn = p.emit("ve", lambda E, c: E.tensor_tensor(
            out=c.sb["kn"].ap().rearrange("p (h k) -> p h k", k=64),
            in0=c.sb["kt"].ap().rearrange("p (h k) -> p h k", k=64),
            in1=c.sb["sgrkn"].ap().unsqueeze(2).broadcast_to([128, 8, 64]), op=ALU.mult), inc="v")
        tks = p.emit("ve", lambda E, c: E.tensor_tensor(
            out=c.sb["ks"].ap().rearrange("p (h k) -> p h k", k=64),
            in0=c.sb["kt"].ap().rearrange("p (h k) -> p h k", k=64),
            in1=c.sb["sgrks"].ap().unsqueeze(2).broadcast_to([128, 8, 64]), op=ALU.mult), inc="v")

        p.dep("te", tqq)
        p.dep("te", tkn)
        for (srcb, dstb, mklo) in (("qq", "qqf", True), ("kn", "knf", False)):
            for a in range(4):
                bk = 4 + (a % 2)
                tt = p.emit("te", lambda E, c, a=a, bk=bk, srcb=srcb: E.transpose(
                    c.ps["P"].ap()[:, 512 * bk:512 * bk + 64].bitcast(BF16),
                    c.sb[srcb][:, 128 * a:128 * (a + 1)], c.sb["identb"][:]), inc="t")
                p.dep("ve", tt)
                te1 = p.emit("ve", lambda E, c, a=a, bk=bk, dstb=dstb: E.tensor_copy(
                    out=c.sb[dstb][:, 128 * a:128 * (a + 1)],
                    in_=c.ps["P"].ap()[:, 512 * bk:512 * bk + 64].bitcast(BF16)), inc="v")
                p.dep("te", te1)
                if mklo:
                    p.dep("ve", te1)
                    p.emit("ve", lambda E, c, a=a: E.tensor_copy(
                        out=c.sb["qqlo"][:, 128 * a:128 * (a + 1)],
                        in_=c.sb["qqf"][64:128, 128 * a:128 * (a + 1)]), inc="v")
        tok_tp = p.tok("v")

        p.dep("te", tok_tp)
        p.dep("ve", tok_tp)
        for a in range(4):
            m1 = p.emit("te", lambda E, c, a=a: E.matmul(
                c.ps["P"][:, 2048:2176],
                c.sb["knf"][0:64, 128 * a:128 * (a + 1)],
                c.sb["qqf"][0:64, 128 * a:128 * (a + 1)], start=True, stop=True), inc="t")
            m2 = p.emit("te", lambda E, c, a=a: E.matmul(
                c.ps["P"][:, 2560:2688],
                c.sb["knf"][64:128, 128 * a:128 * (a + 1)],
                c.sb["qqf"][64:128, 128 * a:128 * (a + 1)], start=True, stop=True), inc="t")
            p.dep("ve", m1)
            p.dep("ve", m2)
            e1 = p.emit("ve", lambda E, c: E.tensor_tensor(
                out=c.sb["sab"][:, 0:128], in0=c.ps["P"][:, 2048:2176],
                in1=c.sb["tri"][:], op=ALU.mult), inc="v")
            e2 = p.emit("ve", lambda E, c: E.tensor_tensor(
                out=c.sb["sab"][:, 128:256], in0=c.ps["P"][:, 2560:2688],
                in1=c.sb["tri"][:], op=ALU.mult), inc="v")
            p.dep("te", e1)
            p.dep("te", e2)
            o1 = p.emit("te", lambda E, c, a=a: E.matmul(
                c.ps["P"][:, 3072:3136],
                c.sb["qqf"][0:64, 128 * a:128 * (a + 1)],
                c.sb["stb"][:, 128 * a:128 * a + 64], start=True, stop=False), inc="t")
            o2 = p.emit("te", lambda E, c, a=a: E.matmul(
                c.ps["P"][:, 3072:3136],
                c.sb["sab"][:, 0:128],
                c.sb["vb"][:, 128 * a:128 * a + 64], start=False, stop=True), inc="t")
            o3 = p.emit("te", lambda E, c, a=a: E.matmul(
                c.ps["P"][:, 3584:3648],
                c.sb["qqlo"][:, 128 * a:128 * (a + 1)],
                c.sb["stb"][:, 128 * a + 64:128 * (a + 1)], start=True, stop=False), inc="t")
            o4 = p.emit("te", lambda E, c, a=a: E.matmul(
                c.ps["P"][:, 3584:3648],
                c.sb["sab"][:, 128:256],
                c.sb["vb"][:, 128 * a + 64:128 * (a + 1)], start=False, stop=True), inc="t")
            d1 = p.emit("te", lambda E, c, a=a: E.matmul(
                c.ps["P"][0:64, 0:64],
                c.sb["ks"][:, 128 * a:128 * a + 64],
                c.sb["vb"][:, 128 * a:128 * a + 64], start=True, stop=True), inc="t")
            d2 = p.emit("te", lambda E, c, a=a: E.matmul(
                c.ps["P"][0:64, 512:576],
                c.sb["ks"][:, 128 * a + 64:128 * (a + 1)],
                c.sb["vb"][:, 128 * a + 64:128 * (a + 1)], start=True, stop=True), inc="t")
            p.dep("ve", o2)
            p.dep("ve", o4)
            r1 = p.emit("ve", lambda E, c, a=a: E.tensor_copy(
                out=c.sb["ret"][:, 128 * a:128 * a + 64], in_=c.ps["P"][:, 3072:3136]), inc="v")
            r2 = p.emit("ve", lambda E, c, a=a: E.tensor_copy(
                out=c.sb["ret"][:, 128 * a + 64:128 * (a + 1)], in_=c.ps["P"][:, 3584:3648]), inc="v")
            p.dep("ve", d1)
            p.dep("ve", d2)
            ua = p.emit("ve", lambda E, c, a=a: E.tensor_tensor(
                out=c.sb["stf"].ap()[:, 128 * a:128 * (a + 1)].rearrange("p (j k) -> p j k", k=64),
                in0=c.sb["stf"].ap()[:, 128 * a:128 * (a + 1)].rearrange("p (j k) -> p j k", k=64),
                in1=c.sb["sg128"].ap()[:, 2 * a:2 * a + 2].unsqueeze(2).broadcast_to([64, 2, 64]),
                op=ALU.mult), inc="v")
            p.dep("ve", ua)
            ub = p.emit("ve", lambda E, c, a=a: E.tensor_tensor(
                out=c.sb["stf"][:, 128 * a:128 * a + 64],
                in0=c.sb["stf"][:, 128 * a:128 * a + 64],
                in1=c.ps["P"][0:64, 0:64], op=ALU.add), inc="v")
            uc = p.emit("ve", lambda E, c, a=a: E.tensor_tensor(
                out=c.sb["stf"][:, 128 * a + 64:128 * (a + 1)],
                in0=c.sb["stf"][:, 128 * a + 64:128 * (a + 1)],
                in1=c.ps["P"][0:64, 512:576], op=ALU.add), inc="v")
            p.dep("ve", ub)
            p.dep("ve", uc)
            ud = p.emit("ve", lambda E, c, a=a: E.tensor_copy(
                out=c.sb["stb"][:, 128 * a:128 * (a + 1)],
                in_=c.sb["stf"][:, 128 * a:128 * (a + 1)]), inc="v")
            p.dep("te", ud)
        tok_ret = p.tok("v")

        p.dep("ve", tok_ret)
        g1 = p.emit("ve", lambda E, c: E.tensor_tensor(
            out=c.sb["dump"][:, 0:512], in0=c.sb["ret"][:], in1=c.sb["ret"][:], op=ALU.mult), inc="v")
        g2 = p.emit("ve", lambda E, c: E.tensor_reduce(
            out=c.sb["gnm"][:], in_=c.sb["ret"].ap().rearrange("p (h k) -> p h k", k=64),
            axis=AX.X, op=ALU.add), inc="v")
        p.dep("ve", g1)
        g3 = p.emit("ve", lambda E, c: E.tensor_reduce(
            out=c.sb["gnv"][:], in_=c.sb["dump"].ap()[:, 0:512].rearrange("p (h k) -> p h k", k=64),
            axis=AX.X, op=ALU.add), inc="v")
        p.dep("ve", g2)
        g4 = p.emit("ve", lambda E, c: E.tensor_scalar_mul(
            out=c.sb["gnm"][:], in0=c.sb["gnm"][:], scalar1=1.0 / 64), inc="v")
        p.dep("ve", g3)
        p.dep("ve", g4)
        g5 = p.emit("ve", lambda E, c: E.tensor_scalar_mul(
            out=c.sb["gnv"][:], in0=c.sb["gnv"][:], scalar1=1.0 / 64), inc="v")
        p.dep("ve", g5)
        g6 = p.emit("ve", lambda E, c: E.tensor_tensor(
            out=c.sb["dump"][:, 512:520], in0=c.sb["gnm"][:], in1=c.sb["gnm"][:], op=ALU.mult), inc="v")
        p.dep("ve", g6)
        g7 = p.emit("ve", lambda E, c: E.tensor_tensor(
            out=c.sb["gnv"][:], in0=c.sb["gnv"][:], in1=c.sb["dump"][:, 512:520], op=ALU.subtract), inc="v")
        p.dep("ve", g7)
        g8 = p.emit("ve", lambda E, c: E.tensor_scalar_add(
            out=c.sb["gnv"][:], in0=c.sb["gnv"][:], scalar1=1e-5), inc="v")
        p.dep("se", g8)
        g9 = p.emit("se", lambda E, c: E.activation(
            out=c.sb["gnv"][:], in_=c.sb["gnv"][:], func=AF.Sqrt), inc="s")
        p.dep("ve", g9)
        g10 = p.emit("ve", lambda E, c: E.reciprocal(out=c.sb["gnv"][:], in_=c.sb["gnv"][:]), inc="v")
        p.dep("ve", g10)
        g11 = p.emit("ve", lambda E, c: E.tensor_tensor(
            out=c.sb["ret"].ap().rearrange("p (h k) -> p h k", k=64),
            in0=c.sb["ret"].ap().rearrange("p (h k) -> p h k", k=64),
            in1=c.sb["gnm"].ap().unsqueeze(2).broadcast_to([128, 8, 64]), op=ALU.subtract), inc="v")
        p.dep("ve", g11)
        g12 = p.emit("ve", lambda E, c: E.tensor_tensor(
            out=c.sb["ret"].ap().rearrange("p (h k) -> p h k", k=64),
            in0=c.sb["ret"].ap().rearrange("p (h k) -> p h k", k=64),
            in1=c.sb["gnv"].ap().unsqueeze(2).broadcast_to([128, 8, 64]), op=ALU.mult), inc="v")
        p.dep("ve", g12)
        g13 = p.emit("ve", lambda E, c: E.tensor_tensor(
            out=c.sb["ret"][:], in0=c.sb["ret"][:], in1=c.sb["gnwb"][:], op=ALU.mult), inc="v")
        p.dep("ve", g13)
        g14 = p.emit("ve", lambda E, c: E.tensor_tensor(
            out=c.sb["ret"][:], in0=c.sb["ret"][:], in1=c.sb["gnbb"][:], op=ALU.add), inc="v")
        p.dep("ve", g14)
        p.wait("ve", "s")
        g15 = p.emit("ve", lambda E, c: E.tensor_tensor(
            out=c.sb["gy"][:], in0=c.sb["ret"][:], in1=c.sb["gb"][:], op=ALU.mult), inc="v")

        p.dep("te", g15)
        for a in range(4):
            bk = 4 + (a % 2)
            tt = p.emit("te", lambda E, c, a=a, bk=bk: E.transpose(
                c.ps["P"].ap()[:, 512 * bk:512 * bk + 64].bitcast(BF16),
                c.sb["gy"][:, 128 * a:128 * (a + 1)], c.sb["identb"][:]), inc="t")
            p.dep("ve", tt)
            te1 = p.emit("ve", lambda E, c, a=a, bk=bk: E.tensor_copy(
                out=c.sb["gyf"][:, 128 * a:128 * (a + 1)],
                in_=c.ps["P"].ap()[:, 512 * bk:512 * bk + 64].bitcast(BF16)), inc="v")
            p.dep("te", te1)
        tok_gyf = p.tok("v")
        p.dep("te", tok_gyf)
        if prev_pout_ev is not None:
            p.dep("te", prev_pout_ev)
        for K in range(4):
            for n in range(2):
                p.emit("te", lambda E, c, K=K, n=n: E.matmul(
                    c.ps["P"][:, 3072 + 512 * n:3072 + 512 * (n + 1)],
                    c.sb["gyf"][:, 128 * K:128 * (K + 1)],
                    c.sb["wo"][:, D * K + 512 * n:D * K + 512 * (n + 1)],
                    start=(K == 0), stop=(K == 3)), inc="t")
        tok_wo = p.tok("t")
        p.dep("ve", tok_wo)
        pe1 = p.emit("ve", lambda E, c: E.tensor_copy(
            out=c.sb["outt"][:, 0:D], in_=c.ps["P"][:, 3072:4096]), inc="v")
        prev_pout_ev = pe1
        p.dep("ge", pe1)
        p.wait("ge", "dg")
        p.emit("ge", lambda E, c, ci=ci: E.dma_start(
            out=c.dram["pout"][128 * ci:128 * (ci + 1), :], in_=c.sb["outt"][:, 0:D]), inc="dg", inc_by=16)
        p.wait("ve", "dg")
        p.wait("te", "v")

    p.wait("ge", "dg")
    p.wait("sy", "dg")

    nc = p.build(nc, sbufs, psums, c)
    return nc


def host_B_inputs(x2, Wq, Wk, Wv, Wg, Wo, gn_w, gn_b):
    H, DK = 16, 64
    pos = np.arange(S, dtype=np.float64)
    inv_freq = 1.0 / (10000.0 ** (np.arange(0, DK, 2, dtype=np.float64) / DK))
    freqs = pos[:, None] * inv_freq[None, :]
    emb = np.concatenate([freqs, freqs], axis=-1)
    cosT = np.cos(emb).astype(np.float32)
    sinT = np.sin(emb).astype(np.float32)
    pp = np.arange(128, dtype=np.float64)
    maps = []
    for cid in range(NC):
        b = cid // 2
        hh = (cid % 2) * 8
        heads = np.arange(hh, hh + 8)
        gammas = 1.0 - 2.0 ** (-5.0 - heads.astype(np.float64))
        grq = (gammas[None, :] ** pp[:, None] * 0.125).astype(np.float32)
        grkn = (gammas[None, :] ** (-pp[:, None])).astype(np.float32)
        grks = (gammas[None, :] ** (128.0 - pp[:, None])).astype(np.float32)
        g128 = np.zeros((64, 8), dtype=np.float32)
        for j in range(8):
            g128[:, j] = (1.0 - 2.0 ** (-5.0 - (hh + j))) ** 128
        tri01 = (np.arange(128)[None, :] >= np.arange(128)[:, None]).astype(np.float32)
        sl = slice(64 * hh, 64 * (hh + 8))
        maps.append({
            "x2b": np.ascontiguousarray(x2[b].astype(np.float32)),
            "wqT": np.ascontiguousarray(Wq[sl, :].T.astype(np.float32)),
            "wkT": np.ascontiguousarray(Wk[sl, :].T.astype(np.float32)),
            "wvT": np.ascontiguousarray(Wv[sl, :].T.astype(np.float32)),
            "wgT": np.ascontiguousarray(Wg[sl, :].T.astype(np.float32)),
            "woTs": np.ascontiguousarray(Wo[:, sl].T.astype(np.float32)),
            "gnw": np.ascontiguousarray(gn_w[sl].astype(np.float32).reshape(1, 512)),
            "gnb": np.ascontiguousarray(gn_b[sl].astype(np.float32).reshape(1, 512)),
            "cosT": cosT, "sinT": sinT,
            "grq": grq, "grkn": grkn, "grks": grks, "g128": g128,
            "tri01": tri01,
        })
    return maps


def run_B(x2, Wq, Wk, Wv, Wg, Wo, gn_w, gn_b):
    nc = build_B()
    res = run_bass_kernel_spmd(nc, host_B_inputs(x2, Wq, Wk, Wv, Wg, Wo, gn_w, gn_b),
                               core_ids=list(range(NC)))
    return [res.results[cid]["pout"] for cid in range(NC)]


LOC = 2688
NLC = LOC // 128


def build_CD(stage):
    nc = bass.Bass("TRN2", target_bir_lowering=False, debug=False)
    p = Prog()
    c = Ctx()

    dram = {}
    if stage == 1:
        dram["x2r"] = nc.dram_tensor("x2r", [LOC, D], F32, kind="ExternalInput").ap()
        dram["p0r"] = nc.dram_tensor("p0r", [LOC, D], F32, kind="ExternalInput").ap()
        dram["p1r"] = nc.dram_tensor("p1r", [LOC, D], F32, kind="ExternalInput").ap()
        dram["x3o"] = nc.dram_tensor("x3o", [2048, D], F32, kind="ExternalOutput").ap()
        dram["y1o"] = nc.dram_tensor("y1o", [2048, D], F32, kind="ExternalOutput").ap()
    else:
        dram["yinr"] = nc.dram_tensor("yinr", [LOC, D], F32, kind="ExternalInput").ap()
        dram["x3r"] = nc.dram_tensor("x3r", [2048, D], F32, kind="ExternalInput").ap()
        dram["outr"] = nc.dram_tensor("outr", [2048, D], F32, kind="ExternalOutput").ap()
    dram["wqkv"] = nc.dram_tensor("wqkv", [D, 1536], F32, kind="ExternalInput").ap()
    dram["qs"] = nc.dram_tensor("qs", [1, 64], F32, kind="ExternalInput").ap()
    dram["ks"] = nc.dram_tensor("ks", [1, 64], F32, kind="ExternalInput").ap()
    dram["wol"] = nc.dram_tensor("wol", [512, D], F32, kind="ExternalInput").ap()
    dram["lg"] = nc.dram_tensor("lg", [1, D], F32, kind="ExternalInput").ap()
    dram["lb"] = nc.dram_tensor("lb", [1, D], F32, kind="ExternalInput").ap()
    dram["w1"] = nc.dram_tensor("w1", [D, 5460], F32, kind="ExternalInput").ap()
    dram["b1"] = nc.dram_tensor("b1", [1, 5460], F32, kind="ExternalInput").ap()
    dram["w2"] = nc.dram_tensor("w2", [2730, D], F32, kind="ExternalInput").ap()
    dram["b2"] = nc.dram_tensor("b2", [1, D], F32, kind="ExternalInput").ap()
    for t in ("c1q", "c2q", "c1k", "c2k"):
        dram[t] = nc.dram_tensor(t, [LOC, 64], F32, kind="ExternalInput").ap()
    dram["amask"] = nc.dram_tensor("amask", [6 * 128, 640], F32, kind="ExternalInput").ap()
    dram["zrow1"] = nc.dram_tensor("zrow1", [1, D], F32, kind="ExternalInput").ap()
    dram["y0d"] = nc.dram_tensor("y0d", [LOC + 1, D], BF16).ap()
    dram["y0f"] = nc.dram_tensor("y0f", [LOC, D], F32).ap()
    dram["y1d"] = nc.dram_tensor("y1d", [2176, D], F32).ap()
    dram["lnh2d"] = nc.dram_tensor("lnh2d", [2048, D], BF16).ap()
    dram["qd"] = nc.dram_tensor("qd", [LOC, 512], BF16).ap()
    dram["kd"] = nc.dram_tensor("kd", [LOC, 512], BF16).ap()
    dram["prodd"] = nc.dram_tensor("prodd", [2048, 2816], BF16).ap()
    c.dram = dram

    sbufs = {
        "xin": ((128, 2 * D), F32),
        "xin2": ((128, D), F32),
        "stat": ((128, 2 * NLC), F32),
        "mu": ((128, NLC), F32),
        "rr": ((128, NLC), F32),
        "dump": ((128, D), F32),
        "lnt": ((128, D), BF16),
        "wob": ((128, 4 * D), BF16),
        "w2d": ((128, 2 * D), BF16),
        "qsb": ((128, 64), F32),
        "ksb": ((128, 64), F32),
        "lgb": ((128, D), F32),
        "lbb": ((128, D), F32),
        "b2b": ((128, D), F32),
        "b1b": ((128, 1024), F32),
        "onesr": ((1, 1280), F32),
        "qt": ((128, 512), F32),
        "kt": ((128, 512), F32),
        "rtmp": ((128, 512), F32),
        "nrm": ((128, 8), F32),
        "qb": ((128, 512), BF16),
        "kb": ((128, 512), BF16),
        "vtm": ((128, NLC * 512), BF16),
        "otm": ((128, NLC * 512), BF16),
        "sf32": ((128, 640), F32),
        "pbf": ((128, 640), BF16),
        "ptb": ((128, 640), BF16),
        "zx": ((128, 4), F32),
        "identb": ((128, 128), BF16),
        "gel": ((128, 512), BF16),
        "outt": ((128, D), F32),
        "yres": ((128, D), F32),
        "tabs": ((128, 4 * 672), F32),
        "arena": ((128, 40960), BF16),
    }
    psums = {"P": ((128, 4096), F32)}

    for s_ in ("dy", "dg", "t", "v", "s", "g"):
        p.sem(s_)

    AR_QF = 0
    AR_KF = 10752
    AR_TAB = 21504
    AR_MSK = 32256

    def qfm(c, a):
        return c.sb["arena"].ap()[:, AR_QF + 2688 * a:AR_QF + 2688 * (a + 1)]

    def kfm(c, a):
        return c.sb["arena"].ap()[:, AR_KF + 2688 * a:AR_KF + 2688 * (a + 1)]

    def tab(c, idx):
        return c.sb["tabs"].ap()[:, 672 * idx:672 * (idx + 1)]

    def msk(c, v):
        base = AR_MSK + v * 1280
        return c.sb["arena"].ap()[:, base:base + 1280].bitcast(F32)


    def lnh2fm(c, k):
        return c.sb["arena"].ap()[:, 2048 * k:2048 * (k + 1)]

    def prodq(c, k):
        return c.sb["arena"].ap()[:, 16384 + 512 * k:16384 + 512 * (k + 1)]

    def w1sl(c, k):
        return c.sb["arena"].ap()[:, 27648 + 1024 * k:27648 + 1024 * (k + 1)]

    def wqkvb(c, k):
        return c.sb["arena"].ap()[:, 21504 + 1536 * k:21504 + 1536 * (k + 1)]

    for idx, t in enumerate(("c1q", "c2q", "c1k", "c2k")):
        p.emit("sy", lambda E, c, idx=idx, t=t: E.dma_start(
            out=tab(c, idx).rearrange("p (n k) -> p n k", k=32),
            in_=c.dram[t].rearrange("(n p) k -> p n k", p=128)[:, :, 0:32]), inc="dy", inc_by=16)
    for k in range(8):
        p.emit("ge", lambda E, c, k=k: E.dma_start(
            out=wqkvb(c, k),
            in_=c.dram["wqkv"][128 * k:128 * (k + 1), :]), inc="dg", inc_by=16)
    for k in range(4):
        p.emit("ge", lambda E, c, k=k: E.dma_start(
            out=c.sb["wob"][:, D * k:D * (k + 1)],
            in_=c.dram["wol"][128 * k:128 * (k + 1), :]), inc="dg", inc_by=16)
    t_on = p.emit("ge", lambda E, c: E.memset(c.sb["onesr"][:, 0:128], 1.0), inc="g")
    p.wait("te", "g")
    bco = [("qs", "qsb", 64), ("ks", "ksb", 64), ("lg", "lgb", D), ("lb", "lbb", D), ("b2", "b2b", D)]
    for nm, dst, w in bco:
        tld = p.emit("sy", lambda E, c, nm=nm, w=w: E.dma_start(
            out=c.sb["onesr"][:, 128:128 + w], in_=c.dram[nm]), inc="dy", inc_by=16)
        p.dep("te", tld)
        nmm = (w + 511) // 512
        for j in range(nmm):
            ww = min(512, w - 512 * j)
            tm = p.emit("te", lambda E, c, j=j, ww=ww: E.matmul(
                c.ps["P"][:, 512 * j:512 * j + ww], c.sb["onesr"][0:1, 0:128],
                c.sb["onesr"][0:1, 128 + 512 * j:128 + 512 * j + ww], start=True, stop=True), inc="t")
        tok = p.tok("t")
        p.dep("ve", tok)
        tv = p.emit("ve", lambda E, c, dst=dst, w=w: E.tensor_copy(
            out=c.sb[dst][:, 0:w], in_=c.ps["P"][:, 0:w]), inc="v")
        p.dep("te", tv)
        p.dep("sy", tv)
    tunb = p.emit("ge", lambda E, c: E.memset(c.sb["identb"][:], 1.0), inc="g")
    p.dep("ge", tunb)
    p.emit("ge", lambda E, c: E.affine_select(
        out=c.sb["identb"][:], in_=c.sb["identb"][:], pattern=[[1, 128]],
        compare_op=ALU.is_equal, fill=0.0, base=0, channel_multiplier=-1), inc="g")
    tz = p.emit("sy", lambda E, c: E.dma_start(out=c.sb["xin2"][0:1, :], in_=c.dram["zrow1"]), inc="dy", inc_by=16)
    p.dep("ve", tz)
    tzb = p.emit("ve", lambda E, c: E.tensor_copy(out=c.sb["lnt"][0:1, :], in_=c.sb["xin2"][0:1, :]), inc="v")
    p.dep("ge", tzb)
    p.emit("ge", lambda E, c: E.dma_start(out=c.dram["y0d"][0:1, :], in_=c.sb["lnt"][0:1, :]), inc="dg", inc_by=16)

    if stage == 1:
        for i in range(NLC):
            p.wait("sy", "v")
            p.wait("sy", "s")
            ta_ = p.emit("sy", lambda E, c, i=i: E.dma_start(
                out=c.sb["xin"][:, 0:D], in_=c.dram["x2r"][128 * i:128 * (i + 1), :]), inc="dy", inc_by=16)
            tb_ = p.emit("sy", lambda E, c, i=i: E.dma_start(
                out=c.sb["xin2"][:], in_=c.dram["p0r"][128 * i:128 * (i + 1), :]), inc="dy", inc_by=16)
            tc_ = p.emit("sy", lambda E, c, i=i: E.dma_start(
                out=c.sb["outt"][:, 0:D], in_=c.dram["p1r"][128 * i:128 * (i + 1), :]), inc="dy", inc_by=16)
            for tt_ in (ta_, tb_, tc_):
                p.dep("ve", tt_)
            v1 = p.emit("ve", lambda E, c: E.tensor_add(
                out=c.sb["xin"][:, 0:D], in0=c.sb["xin"][:, 0:D], in1=c.sb["xin2"][:]), inc="v")
            p.dep("ve", v1)
            v2 = p.emit("ve", lambda E, c: E.tensor_add(
                out=c.sb["xin"][:, 0:D], in0=c.sb["xin"][:, 0:D], in1=c.sb["outt"][:, 0:D]), inc="v")
            p.dep("ve", v2)
            if i >= 5:
                p.dep("ge", v2)
                p.emit("ge", lambda E, c, i=i: E.dma_start(
                    out=c.dram["x3o"][128 * (i - 5):128 * (i - 4), :], in_=c.sb["xin"][:, 0:D]),
                    inc="dg", inc_by=16)
            v3 = p.emit("ve", lambda E, c, i=i: E.reduce_sum(
                out=c.sb["stat"][:, i:i + 1], in_=c.sb["xin"][:, 0:D], axis=AX.X), inc="v")
            p.dep("se", v2)
            p.wait("se", "s")
            v4 = p.emit("se", lambda E, c, i=i: E.activation(
                out=c.sb["dump"][:], in_=c.sb["xin"][:, 0:D], func=AF.Square,
                accum_out=c.sb["stat"][:, NLC + i:NLC + i + 1]), inc="s")
            p.wait("sy", "dg")
        p.wait("ve", "v")
        p.wait("ve", "s")
        t1 = p.emit("ve", lambda E, c: E.tensor_scalar_mul(
            out=c.sb["mu"][:], in0=c.sb["stat"][:, 0:NLC], scalar1=1.0 / D), inc="v")
        p.dep("ve", t1)
        t2 = p.emit("ve", lambda E, c: E.tensor_tensor(
            out=c.sb["rr"][:], in0=c.sb["mu"][:], in1=c.sb["mu"][:], op=ALU.mult), inc="v")
        p.dep("ve", t2)
        t3 = p.emit("ve", lambda E, c: E.scalar_tensor_tensor(
            out=c.sb["rr"][:], in0=c.sb["stat"][:, NLC:2 * NLC], scalar=1.0 / D,
            in1=c.sb["rr"][:], op0=ALU.mult, op1=ALU.subtract), inc="v")
        p.dep("ve", t3)
        t4 = p.emit("ve", lambda E, c: E.tensor_scalar_add(
            out=c.sb["rr"][:], in0=c.sb["rr"][:], scalar1=1e-5), inc="v")
        p.dep("se", t4)
        t5 = p.emit("se", lambda E, c: E.activation(
            out=c.sb["rr"][:], in_=c.sb["rr"][:], func=AF.Sqrt), inc="s")
        p.dep("ve", t5)
        t6 = p.emit("ve", lambda E, c: E.reciprocal(out=c.sb["rr"][:], in_=c.sb["rr"][:]), inc="v")
        tok_stats = p.tok("v")
        prev_w = None
        for i in range(NLC):
            p.wait("sy", "v")
            ta_ = p.emit("sy", lambda E, c, i=i: E.dma_start(
                out=c.sb["xin"][:, 0:D], in_=c.dram["x2r"][128 * i:128 * (i + 1), :]), inc="dy", inc_by=16)
            tb_ = p.emit("sy", lambda E, c, i=i: E.dma_start(
                out=c.sb["xin2"][:], in_=c.dram["p0r"][128 * i:128 * (i + 1), :]), inc="dy", inc_by=16)
            tc_ = p.emit("sy", lambda E, c, i=i: E.dma_start(
                out=c.sb["outt"][:, 0:D], in_=c.dram["p1r"][128 * i:128 * (i + 1), :]), inc="dy", inc_by=16)
            for tt_ in (ta_, tb_, tc_):
                p.dep("ve", tt_)
            p.dep("ve", tok_stats)
            if prev_w is not None:
                p.dep("ve", prev_w)
            v1 = p.emit("ve", lambda E, c: E.tensor_add(
                out=c.sb["xin"][:, 0:D], in0=c.sb["xin"][:, 0:D], in1=c.sb["xin2"][:]), inc="v")
            p.dep("ve", v1)
            v2 = p.emit("ve", lambda E, c: E.tensor_add(
                out=c.sb["xin"][:, 0:D], in0=c.sb["xin"][:, 0:D], in1=c.sb["outt"][:, 0:D]), inc="v")
            p.dep("ve", v2)
            v5 = p.emit("ve", lambda E, c, i=i: E.tensor_scalar(
                out=c.sb["xin"][:, D:2 * D], in0=c.sb["xin"][:, 0:D],
                scalar1=c.sb["mu"][:, i:i + 1], scalar2=c.sb["rr"][:, i:i + 1],
                op0=ALU.subtract, op1=ALU.mult), inc="v")
            p.dep("ve", v5)
            v6 = p.emit("ve", lambda E, c: E.tensor_copy(
                out=c.sb["lnt"][:], in_=c.sb["xin"][:, D:2 * D]), inc="v")
            p.dep("ge", v6)
            w1_ = p.emit("ge", lambda E, c, i=i: E.dma_start(
                out=c.dram["y0f"][128 * i:128 * (i + 1), :], in_=c.sb["xin"][:, D:2 * D]), inc="dg", inc_by=16)
            w2_ = p.emit("ge", lambda E, c, i=i: E.dma_start(
                out=c.dram["y0d"][1 + 128 * i:1 + 128 * (i + 1), :], in_=c.sb["lnt"][:]), inc="dg", inc_by=16)
            prev_w = w2_
    else:
        prev_w = None
        for i in range(NLC):
            p.wait("sy", "v")
            ta_ = p.emit("sy", lambda E, c, i=i: E.dma_start(
                out=c.sb["xin"][:, 0:D], in_=c.dram["yinr"][128 * i:128 * (i + 1), :]), inc="dy", inc_by=16)
            p.dep("ve", ta_)
            if prev_w is not None:
                p.dep("ve", prev_w)
            v6 = p.emit("ve", lambda E, c: E.tensor_copy(
                out=c.sb["lnt"][:], in_=c.sb["xin"][:, 0:D]), inc="v")
            p.dep("ge", v6)
            prev_w = p.emit("ge", lambda E, c, i=i: E.dma_start(
                out=c.dram["y0d"][1 + 128 * i:1 + 128 * (i + 1), :], in_=c.sb["lnt"][:]), inc="dg", inc_by=16)

    def y0f_ap(c, r0, r1):
        if stage == 1:
            return c.dram["y0f"][r0:r1, :]
        return c.dram["yinr"][r0:r1, :]

    def h1fm(c, k):
        return c.sb["arena"].ap()[:, 2688 * k:2688 * (k + 1)]

    p.wait("sy", "dg")
    for k in range(8):
        if k < 4:
            p.emit("sy", lambda E, c, k=k: E.dma_start_transpose(
                out=h1fm(c, k), in_=c.dram["y0d"][1:1 + LOC, 128 * k:128 * (k + 1)]), inc="dy", inc_by=16)
        else:
            p.emit("sy", lambda E, c, k=k: E.dma_start_transpose(
                out=h1fm(c, k), in_=c.dram["y0d"][0:LOC, 128 * k:128 * (k + 1)]), inc="dy", inc_by=16)
    tok_h1 = p.tok("dy")

    p.dep("te", tok_h1)
    p.wait("te", "dg")
    p.wait("te", "v")
    p.wait("te", "g")
    for ci in range(NLC):
        for K in range(8):
            for n in range(3):
                p.emit("te", lambda E, c, ci=ci, K=K, n=n: E.matmul(
                    c.ps["P"][:, 512 * n:512 * (n + 1)],
                    h1fm(c, K)[:, 128 * ci:128 * (ci + 1)],
                    wqkvb(c, K)[:, 512 * n:512 * (n + 1)],
                    start=(K == 0), stop=(K == 7)), inc="t")
        tok_mm = p.tok("t")
        p.dep("ve", tok_mm)
        tq = p.emit("ve", lambda E, c: E.tensor_copy(out=c.sb["qt"][:], in_=c.ps["P"][:, 0:512]), inc="v")
        tk = p.emit("ve", lambda E, c: E.tensor_copy(out=c.sb["kt"][:], in_=c.ps["P"][:, 512:1024]), inc="v")
        tv_ = p.emit("ve", lambda E, c, ci=ci: E.tensor_copy(
            out=c.sb["vtm"][:, 512 * ci:512 * (ci + 1)], in_=c.ps["P"][:, 1024:1536]), inc="v")
        p.dep("te", tv_)

        def prep(src, tabc, tabs, scaleb, dep0):
            r3 = lambda ap: ap.rearrange("p (h k) -> p h k", k=64)
            p.dep("ve", dep0)
            a1 = p.emit("ve", lambda E, c, src=src, r3=r3: E.tensor_tensor(
                out=c.sb["rtmp"][:], in0=c.sb[src][:], in1=c.sb[src][:], op=ALU.mult), inc="v")
            p.dep("ve", a1)
            a2 = p.emit("ve", lambda E, c, r3=r3: E.tensor_reduce(
                out=c.sb["nrm"][:], in_=r3(c.sb["rtmp"].ap()), axis=AX.X, op=ALU.add), inc="v")
            p.dep("se", a2)
            a3 = p.emit("se", lambda E, c: E.activation(
                out=c.sb["nrm"][:], in_=c.sb["nrm"][:], func=AF.Sqrt), inc="s")
            p.dep("ve", a3)
            a4 = p.emit("ve", lambda E, c: E.tensor_scalar_max(
                out=c.sb["nrm"][:], in0=c.sb["nrm"][:], scalar1=1e-12), inc="v")
            p.dep("ve", a4)
            a5 = p.emit("ve", lambda E, c: E.reciprocal(out=c.sb["nrm"][:], in_=c.sb["nrm"][:]), inc="v")
            p.dep("ve", a5)
            a6 = p.emit("ve", lambda E, c, src=src, r3=r3: E.tensor_tensor(
                out=r3(c.sb[src].ap()), in0=r3(c.sb[src].ap()),
                in1=c.sb["nrm"].ap().unsqueeze(2).broadcast_to([128, 8, 64]), op=ALU.mult), inc="v")
            p.dep("ve", a6)
            a7 = p.emit("ve", lambda E, c, src=src, scaleb=scaleb, r3=r3: E.tensor_tensor(
                out=r3(c.sb[src].ap()), in0=r3(c.sb[src].ap()),
                in1=c.sb[scaleb].ap()[:, 0:64].unsqueeze(1).broadcast_to([128, 8, 64]), op=ALU.mult), inc="v")
            co = 64 * 0
            p.dep("ve", a7)
            r4 = lambda ap: ap.rearrange("p (h two k) -> p h two k", two=2, k=32)
            b1_ = p.emit("ve", lambda E, c, src=src, tabc=tabc, ci2=ci, r4=r4: E.tensor_tensor(
                out=r4(c.sb["rtmp"].ap()), in0=r4(c.sb[src].ap()),
                in1=tab(c, tabc)[:, 32 * ci2:32 * ci2 + 32].unsqueeze(1).unsqueeze(2).broadcast_to([128, 8, 2, 32]),
                op=ALU.mult), inc="v")
            b2_ = p.emit("ve", lambda E, c, src=src, tabs=tabs, ci2=ci, r3=r3: E.tensor_tensor(
                out=r3(c.sb["dump"].ap()[:, 0:512])[:, :, 32:64],
                in0=r3(c.sb[src].ap())[:, :, 0:32],
                in1=tab(c, tabs)[:, 32 * ci2:32 * ci2 + 32].unsqueeze(1).broadcast_to([128, 8, 32]),
                op=ALU.mult), inc="v")
            p.dep("ve", b2_)
            b3_ = p.emit("ve", lambda E, c, src=src, tabs=tabs, ci2=ci, r3=r3: E.tensor_tensor(
                out=r3(c.sb[src].ap())[:, :, 0:32],
                in0=r3(c.sb[src].ap())[:, :, 32:64],
                in1=tab(c, tabs)[:, 32 * ci2:32 * ci2 + 32].unsqueeze(1).broadcast_to([128, 8, 32]),
                op=ALU.mult), inc="v")
            p.dep("ve", b1_)
            p.dep("ve", b3_)
            b4_ = p.emit("ve", lambda E, c, src=src, r3=r3: E.tensor_tensor(
                out=r3(c.sb[src].ap())[:, :, 0:32],
                in0=r3(c.sb["rtmp"].ap())[:, :, 0:32],
                in1=r3(c.sb[src].ap())[:, :, 0:32], op=ALU.subtract), inc="v")
            p.dep("ve", b4_)
            b5_ = p.emit("ve", lambda E, c, src=src, r3=r3: E.tensor_tensor(
                out=r3(c.sb[src].ap())[:, :, 32:64],
                in0=r3(c.sb["rtmp"].ap())[:, :, 32:64],
                in1=r3(c.sb["dump"].ap()[:, 0:512])[:, :, 32:64], op=ALU.add), inc="v")
            p.dep("ve", b5_)
            dstb = "qb" if src == "qt" else "kb"
            b6_ = p.emit("ve", lambda E, c, src=src, dstb=dstb: E.tensor_copy(
                out=c.sb[dstb][:], in_=c.sb[src][:]), inc="v")
            return b6_

        tqp = prep("qt", 0, 1, "qsb", tq)
        tkp = prep("kt", 2, 3, "ksb", tk)
        p.dep("ge", tqp)
        p.dep("ge", tkp)
        p.emit("ge", lambda E, c, ci=ci: E.dma_start(
            out=c.dram["qd"][128 * ci:128 * (ci + 1), :], in_=c.sb["qb"][:]), inc="dg", inc_by=16)
        p.emit("ge", lambda E, c, ci=ci: E.dma_start(
            out=c.dram["kd"][128 * ci:128 * (ci + 1), :], in_=c.sb["kb"][:]), inc="dg", inc_by=16)
        p.wait("ve", "dg")

    p.wait("sy", "dg")
    p.wait("sy", "t")
    for a in range(4):
        p.emit("sy", lambda E, c, a=a: E.dma_start_transpose(
            out=qfm(c, a), in_=c.dram["qd"][:, 128 * a:128 * (a + 1)]), inc="dy", inc_by=16)
        p.emit("sy", lambda E, c, a=a: E.dma_start_transpose(
            out=kfm(c, a), in_=c.dram["kd"][:, 128 * a:128 * (a + 1)]), inc="dy", inc_by=16)
    for v in range(6):
        p.emit("sy", lambda E, c, v=v: E.dma_start(
            out=msk(c, v)[:, 0:640], in_=c.dram["amask"][128 * v:128 * (v + 1), :]),
            inc="dy", inc_by=16)
    tok_prep = p.tok("dy")

    p.dep("te", tok_prep)
    for ci in range(4, NLC):
        mv = min(max(ci - 4, 0), 5)
        for a in range(4):
            for half, base in ((0, 0), (1, 2)):
                rb = 64 * half
                for n in range(2):
                    w = 512 if n == 0 else 128
                    p.emit("te", lambda E, c, a=a, ci=ci, half=half, base=base, n=n, w=w, rb=rb: E.matmul(
                        c.ps["P"][:, 512 * base + 512 * n:512 * base + 512 * n + w],
                        qfm(c, a)[rb:rb + 64, 128 * ci:128 * (ci + 1)],
                        kfm(c, a)[rb:rb + 64, 128 * (ci - 4) + 512 * n:128 * (ci - 4) + 512 * n + w],
                        start=True, stop=True), inc="t")
            tok_s = p.tok("t")
            for half, base in ((0, 0), (1, 2)):
                hd = 2 * a + half
                p.dep("ve", tok_s)
                s1 = p.emit("ve", lambda E, c, base=base, mv=mv: E.tensor_tensor(
                    out=c.sb["sf32"][:], in0=c.ps["P"][:, 512 * base:512 * base + 640],
                    in1=msk(c, mv)[:, 0:640], op=ALU.add), inc="v")
                p.dep("ve", s1)
                s2 = p.emit("ve", lambda E, c: E.reduce_max(
                    out=c.sb["zx"][:, 0:1], in_=c.sb["sf32"][:], axis=AX.X), inc="v")
                p.dep("ve", s2)
                s3 = p.emit("ve", lambda E, c: E.tensor_scalar_mul(
                    out=c.sb["zx"][:, 1:2], in0=c.sb["zx"][:, 0:1], scalar1=-1.0), inc="v")
                p.dep("se", s3)
                s4 = p.emit("se", lambda E, c: E.activation(
                    out=c.sb["pbf"][:], in_=c.sb["sf32"][:], func=AF.Exp,
                    bias=c.sb["zx"][:, 1:2], accum_out=c.sb["zx"][:, 2:3]), inc="s")
                p.dep("ve", s4)
                s5 = p.emit("ve", lambda E, c: E.reciprocal(
                    out=c.sb["zx"][:, 3:4], in_=c.sb["zx"][:, 2:3]), inc="v")
                p.dep("te", s4)
                for tk5 in range(5):
                    bk = 4 + (tk5 % 2)
                    tt = p.emit("te", lambda E, c, tk5=tk5, bk=bk: E.transpose(
                        c.ps["P"].ap()[:, 512 * bk:512 * bk + 64].bitcast(BF16),
                        c.sb["pbf"][:, 128 * tk5:128 * (tk5 + 1)], c.sb["identb"][:]), inc="t")
                    p.dep("ve", tt)
                    te1 = p.emit("ve", lambda E, c, tk5=tk5, bk=bk: E.tensor_copy(
                        out=c.sb["ptb"][:, 128 * tk5:128 * (tk5 + 1)],
                        in_=c.ps["P"].ap()[:, 512 * bk:512 * bk + 64].bitcast(BF16)), inc="v")
                    p.dep("te", te1)
                tok_pt = p.tok("v")
                p.dep("te", tok_pt)
                for tk5 in range(5):
                    p.emit("te", lambda E, c, tk5=tk5, ci=ci, hd=hd: E.matmul(
                        c.ps["P"][:, 3584:3648],
                        c.sb["ptb"][:, 128 * tk5:128 * (tk5 + 1)],
                        c.sb["vtm"][:, 512 * (ci - 4 + tk5) + 64 * hd:512 * (ci - 4 + tk5) + 64 * (hd + 1)],
                        start=(tk5 == 0), stop=(tk5 == 4)), inc="t")
                tok_o = p.tok("t")
                p.dep("ve", tok_o)
                p.dep("ve", s5)
                oe = p.emit("ve", lambda E, c, ci=ci, hd=hd: E.tensor_scalar_mul(
                    out=c.sb["otm"][:, 512 * ci + 64 * hd:512 * ci + 64 * (hd + 1)],
                    in0=c.ps["P"][:, 3584:3648], scalar1=c.sb["zx"][:, 3:4]), inc="v")
                p.dep("te", oe)
            p.wait("te", "v")

    prev_y1w = None
    for ci in range(4, NLC):
        for a in range(4):
            bk = 4 + (a % 2)
            tt = p.emit("te", lambda E, c, ci=ci, a=a, bk=bk: E.transpose(
                c.ps["P"].ap()[:, 512 * bk:512 * bk + 64].bitcast(BF16),
                c.sb["otm"][:, 512 * ci + 128 * a:512 * ci + 128 * (a + 1)], c.sb["identb"][:]), inc="t")
            p.dep("ve", tt)
            te1 = p.emit("ve", lambda E, c, a=a, bk=bk: E.tensor_copy(
                out=c.sb["ptb"][:, 128 * a:128 * (a + 1)],
                in_=c.ps["P"].ap()[:, 512 * bk:512 * bk + 64].bitcast(BF16)), inc="v")
            p.dep("te", te1)
        tok_of = p.tok("v")
        p.dep("te", tok_of)
        for K in range(4):
            for n in range(2):
                p.emit("te", lambda E, c, K=K, n=n: E.matmul(
                    c.ps["P"][:, 3072 + 512 * n:3072 + 512 * (n + 1)],
                    c.sb["ptb"][:, 128 * K:128 * (K + 1)],
                    c.sb["wob"][:, D * K + 512 * n:D * K + 512 * (n + 1)],
                    start=(K == 0), stop=(K == 3)), inc="t")
        tok_wo = p.tok("t")
        p.wait("sy", "v")
        tldy = p.emit("sy", lambda E, c, ci=ci: E.dma_start(
            out=c.sb["yres"][:, 0:D], in_=y0f_ap(c, 128 * ci, 128 * (ci + 1))), inc="dy", inc_by=16)
        p.dep("ve", tok_wo)
        p.dep("ve", tldy)
        if prev_y1w is not None:
            p.dep("ve", prev_y1w)
        ev = p.emit("ve", lambda E, c: E.tensor_add(
            out=c.sb["outt"][:, 0:D], in0=c.ps["P"].ap()[:, 3072:4096],
            in1=c.sb["yres"][:, 0:D]), inc="v")
        p.dep("ge", ev)
        p.dep("te", ev)
        wy = p.emit("ge", lambda E, c, ci=ci: E.dma_start(
            out=c.dram["y1d"][128 * (ci - 4):128 * (ci - 3), :], in_=c.sb["outt"][:, 0:D]), inc="dg", inc_by=16)
        prev_y1w = wy
        if stage == 1 and ci >= 5:
            p.dep("ge", ev)
            p.emit("ge", lambda E, c, ci=ci: E.dma_start(
                out=c.dram["y1o"][128 * (ci - 5):128 * (ci - 4), :], in_=c.sb["outt"][:, 0:D]), inc="dg", inc_by=16)

    p.wait("sy", "dg")
    for i in range(16):
        p.wait("sy", "v")
        p.wait("sy", "s")
        ta_ = p.emit("sy", lambda E, c, i=i: E.dma_start(
            out=c.sb["xin"][:, 0:512], in_=c.dram["y1d"][128 + 128 * i:256 + 128 * i, 0:512]), inc="dy", inc_by=16)
        tb_ = p.emit("sy", lambda E, c, i=i: E.dma_start(
            out=c.sb["xin"][:, 512:1024], in_=c.dram["y1d"][127 + 128 * i:255 + 128 * i, 512:1024]), inc="dy", inc_by=16)
        p.dep("ve", ta_)
        p.dep("ve", tb_)
        v3 = p.emit("ve", lambda E, c, i=i: E.reduce_sum(
            out=c.sb["stat"][:, i:i + 1], in_=c.sb["xin"][:, 0:D], axis=AX.X), inc="v")
        p.dep("se", ta_)
        p.dep("se", tb_)
        v4 = p.emit("se", lambda E, c, i=i: E.activation(
            out=c.sb["dump"][:], in_=c.sb["xin"][:, 0:D], func=AF.Square,
            accum_out=c.sb["stat"][:, NLC + i:NLC + i + 1]), inc="s")
        p.wait("se", "s")
    p.wait("ve", "v")
    p.wait("ve", "s")
    t1 = p.emit("ve", lambda E, c: E.tensor_scalar_mul(
        out=c.sb["mu"][:, 0:16], in0=c.sb["stat"][:, 0:16], scalar1=1.0 / D), inc="v")
    p.dep("ve", t1)
    t2 = p.emit("ve", lambda E, c: E.tensor_tensor(
        out=c.sb["rr"][:, 0:16], in0=c.sb["mu"][:, 0:16], in1=c.sb["mu"][:, 0:16], op=ALU.mult), inc="v")
    p.dep("ve", t2)
    t3 = p.emit("ve", lambda E, c: E.scalar_tensor_tensor(
        out=c.sb["rr"][:, 0:16], in0=c.sb["stat"][:, NLC:NLC + 16], scalar=1.0 / D,
        in1=c.sb["rr"][:, 0:16], op0=ALU.mult, op1=ALU.subtract), inc="v")
    p.dep("ve", t3)
    t4 = p.emit("ve", lambda E, c: E.tensor_scalar_add(
        out=c.sb["rr"][:, 0:16], in0=c.sb["rr"][:, 0:16], scalar1=1e-5), inc="v")
    p.dep("se", t4)
    t5 = p.emit("se", lambda E, c: E.activation(
        out=c.sb["rr"][:, 0:16], in_=c.sb["rr"][:, 0:16], func=AF.Sqrt), inc="s")
    p.dep("ve", t5)
    t6 = p.emit("ve", lambda E, c: E.reciprocal(out=c.sb["rr"][:, 0:16], in_=c.sb["rr"][:, 0:16]), inc="v")
    tok_st2 = p.tok("v")
    prev_w = None
    for i in range(16):
        p.wait("sy", "v")
        ta_ = p.emit("sy", lambda E, c, i=i: E.dma_start(
            out=c.sb["xin"][:, 0:512], in_=c.dram["y1d"][128 + 128 * i:256 + 128 * i, 0:512]), inc="dy", inc_by=16)
        tb_ = p.emit("sy", lambda E, c, i=i: E.dma_start(
            out=c.sb["xin"][:, 512:1024], in_=c.dram["y1d"][127 + 128 * i:255 + 128 * i, 512:1024]), inc="dy", inc_by=16)
        p.dep("ve", ta_)
        p.dep("ve", tb_)
        p.dep("ve", tok_st2)
        if prev_w is not None:
            p.dep("ve", prev_w)
        v5 = p.emit("ve", lambda E, c, i=i: E.tensor_scalar(
            out=c.sb["xin"][:, D:2 * D], in0=c.sb["xin"][:, 0:D],
            scalar1=c.sb["mu"][:, i:i + 1], scalar2=c.sb["rr"][:, i:i + 1],
            op0=ALU.subtract, op1=ALU.mult), inc="v")
        p.dep("ve", v5)
        v6 = p.emit("ve", lambda E, c: E.tensor_tensor(
            out=c.sb["xin"][:, D:2 * D], in0=c.sb["xin"][:, D:2 * D], in1=c.sb["lgb"][:], op=ALU.mult), inc="v")
        p.dep("ve", v6)
        v7 = p.emit("ve", lambda E, c: E.scalar_tensor_tensor(
            out=c.sb["lnt"][:], in0=c.sb["xin"][:, D:2 * D], scalar=1.0,
            in1=c.sb["lbb"][:], op0=ALU.mult, op1=ALU.add), inc="v")
        p.dep("ge", v7)
        prev_w = p.emit("ge", lambda E, c, i=i: E.dma_start(
            out=c.dram["lnh2d"][128 * i:128 * (i + 1), :], in_=c.sb["lnt"][:]), inc="dg", inc_by=16)

    p.wait("sy", "dg")
    p.wait("sy", "t")
    p.wait("sy", "v")
    for k in range(8):
        p.emit("sy", lambda E, c, k=k: E.dma_start_transpose(
            out=lnh2fm(c, k), in_=c.dram["lnh2d"][:, 128 * k:128 * (k + 1)]), inc="dy", inc_by=16)
    tok_ln2 = p.tok("dy")
    for pair in range(6):
        W = 512 if pair < 5 else 170
        a0 = 512 * pair
        g0 = 2730 + 512 * pair
        p.wait("ge", "t")
        for K in range(8):
            p.emit("ge", lambda E, c, K=K, a0=a0, W=W: E.dma_start(
                out=w1sl(c, K)[:, 0:W],
                in_=c.dram["w1"][128 * K:128 * (K + 1), a0:a0 + W]), inc="dg", inc_by=16)
            p.emit("ge", lambda E, c, K=K, g0=g0, W=W: E.dma_start(
                out=w1sl(c, K)[:, 512:512 + W],
                in_=c.dram["w1"][128 * K:128 * (K + 1), g0:g0 + W]), inc="dg", inc_by=16)
        p.wait("sy", "t")
        p.wait("sy", "v")
        tba = p.emit("sy", lambda E, c, a0=a0, W=W: E.dma_start(
            out=c.sb["onesr"][:, 128:128 + W], in_=c.dram["b1"][0:1, a0:a0 + W]), inc="dy", inc_by=16)
        tbg = p.emit("sy", lambda E, c, g0=g0, W=W: E.dma_start(
            out=c.sb["onesr"][:, 768:768 + W], in_=c.dram["b1"][0:1, g0:g0 + W]), inc="dy", inc_by=16)
        p.dep("te", tba)
        p.dep("te", tbg)
        p.wait("te", "v")
        tm1 = p.emit("te", lambda E, c, W=W: E.matmul(
            c.ps["P"][:, 0:W], c.sb["onesr"][0:1, 0:128], c.sb["onesr"][0:1, 128:128 + W],
            start=True, stop=True), inc="t")
        tm2 = p.emit("te", lambda E, c, W=W: E.matmul(
            c.ps["P"][:, 512:512 + W], c.sb["onesr"][0:1, 0:128], c.sb["onesr"][0:1, 768:768 + W],
            start=True, stop=True), inc="t")
        p.dep("ve", tm2)
        p.wait("ve", "t")
        tbca = p.emit("ve", lambda E, c, W=W: E.tensor_copy(
            out=c.sb["b1b"][:, 0:W], in_=c.ps["P"][:, 0:W]), inc="v")
        p.dep("ve", tbca)
        tbc = p.emit("ve", lambda E, c, W=W: E.tensor_copy(
            out=c.sb["b1b"][:, 512:512 + W], in_=c.ps["P"][:, 512:512 + W]), inc="v")
        p.dep("te", tbc)
        p.dep("te", tok_ln2)
        p.wait("te", "dg")
        for t in range(16):
            for K in range(8):
                p.emit("te", lambda E, c, t=t, K=K, W=W: E.matmul(
                    c.ps["P"][:, 1024:1024 + W],
                    lnh2fm(c, K)[:, 128 * t:128 * (t + 1)],
                    w1sl(c, K)[:, 0:W],
                    start=(K == 0), stop=(K == 7)), inc="t")
                p.emit("te", lambda E, c, t=t, K=K, W=W: E.matmul(
                    c.ps["P"][:, 2048:2048 + W],
                    lnh2fm(c, K)[:, 128 * t:128 * (t + 1)],
                    w1sl(c, K)[:, 512:512 + W],
                    start=(K == 0), stop=(K == 7)), inc="t")
            tok_w1 = p.tok("t")
            p.dep("ve", tok_w1)
            p.wait("ve", "s")
            e1 = p.emit("ve", lambda E, c, W=W: E.scalar_tensor_tensor(
                out=c.sb["sf32"][:, 0:W], in0=c.ps["P"][:, 2048:2048 + W], scalar=1.0,
                in1=c.sb["b1b"][:, 512:512 + W], op0=ALU.mult, op1=ALU.add), inc="v")
            p.dep("se", e1)
            sg = p.emit("se", lambda E, c, W=W: E.activation(
                out=c.sb["gel"][:, 0:W], in_=c.sb["sf32"][:, 0:W], func=AF.Gelu), inc="s")
            p.dep("ve", e1)
            p.dep("ve", sg)
            e2 = p.emit("ve", lambda E, c, W=W: E.scalar_tensor_tensor(
                out=c.sb["sf32"][:, 0:W], in0=c.ps["P"][:, 1024:1024 + W], scalar=1.0,
                in1=c.sb["b1b"][:, 0:W], op0=ALU.mult, op1=ALU.add), inc="v")
            p.dep("te", e2)
            p.wait("ve", "dg")
            e3 = p.emit("ve", lambda E, c, W=W: E.tensor_tensor(
                out=c.sb["lnt"][:, 0:W], in0=c.sb["sf32"][:, 0:W], in1=c.sb["gel"][:, 0:W],
                op=ALU.mult), inc="v")
            p.dep("ge", e3)
            p.emit("ge", lambda E, c, t=t, a0=a0, W=W: E.dma_start(
                out=c.dram["prodd"][128 * t:128 * (t + 1), a0:a0 + W], in_=c.sb["lnt"][:, 0:W]),
                inc="dg", inc_by=16)

    p.wait("ve", "dg")
    tzp = p.emit("ve", lambda E, c: E.memset(c.sb["lnt"][:, 0:86], 0.0), inc="v")
    p.dep("ge", tzp)
    for i in range(16):
        p.emit("ge", lambda E, c, i=i: E.dma_start(
            out=c.dram["prodd"][128 * i:128 * (i + 1), 2730:2816], in_=c.sb["lnt"][:, 0:86]),
            inc="dg", inc_by=16)

    p.wait("sy", "dg")
    p.wait("sy", "t")
    for grp in range(4):
        if grp >= 1:
            p.wait("sy", "t")
        for k in range(22):
            p.emit("sy", lambda E, c, k=k, grp=grp: E.dma_start_transpose(
                out=prodq(c, k),
                in_=c.dram["prodd"][512 * grp:512 * (grp + 1), 128 * k:128 * (k + 1)] if k < 21
                else c.dram["prodd"][512 * grp:512 * (grp + 1), 2688:2816]),
                inc="dy", inc_by=16)
        tok_pq = p.tok("dy")
        p.dep("te", tok_pq)
        p.wait("te", "v")
        prev_wld = {}
        for K in range(22):
            kw = 128 if K < 21 else 42
            wpar = K % 2
            if wpar in prev_wld:
                p.dep("ge", prev_wld[wpar])
            tw2 = p.emit("ge", lambda E, c, K=K, kw=kw, wpar=wpar: E.dma_start(
                out=c.sb["w2d"][0:kw, D * wpar:D * wpar + D],
                in_=c.dram["w2"][128 * K:128 * K + kw, :]), inc="dg", inc_by=16)
            p.dep("te", tw2)
            for t4 in range(4):
                bank = t4 * 1024
                for n in range(2):
                    tmK = p.emit("te", lambda E, c, t4=t4, K=K, kw=kw, n=n, bank=bank, wpar=wpar: E.matmul(
                        c.ps["P"][:, bank + 512 * n:bank + 512 * (n + 1)],
                        prodq(c, K)[0:kw, 128 * t4:128 * (t4 + 1)],
                        c.sb["w2d"][0:kw, D * wpar + 512 * n:D * wpar + 512 * n + 512],
                        start=(K == 0), stop=(K == 21)), inc="t")
            prev_wld[wpar] = tmK
        tok_allmm2 = p.tok("t")
        for t4 in range(4):
            t = grp * 4 + t4
            bank = t4 * 1024
            tok_mm2 = tok_allmm2
            so = 0
            p.wait("sy", "v")
            tldy = p.emit("sy", lambda E, c, t=t: E.dma_start(
                out=c.sb["yres"][:, 0:D], in_=c.dram["y1d"][128 + 128 * t:256 + 128 * t, :]),
                inc="dy", inc_by=16)
            if stage == 2:
                tldx = p.emit("sy", lambda E, c, t=t: E.dma_start(
                    out=c.sb["xin"][:, 0:D], in_=c.dram["x3r"][128 * t:128 * (t + 1), :]),
                    inc="dy", inc_by=16)
            p.dep("ve", tok_mm2)
            p.dep("ve", tldy)
            if stage == 2:
                p.dep("ve", tldx)
            p.wait("ve", "dg")
            p.wait("ve", "dg")
            ev1 = p.emit("ve", lambda E, c, bank=bank: E.scalar_tensor_tensor(
                out=c.sb["outt"][:, 0:D], in0=c.ps["P"].ap()[:, bank:bank + 1024],
                scalar=1.0, in1=c.sb["b2b"][:], op0=ALU.mult, op1=ALU.add), inc="v")
            p.dep("ve", ev1)
            ev2 = p.emit("ve", lambda E, c: E.tensor_add(
                out=c.sb["outt"][:, 0:D], in0=c.sb["outt"][:, 0:D],
                in1=c.sb["yres"][:, 0:D]), inc="v")
            p.dep("te", ev2)
            if stage == 2:
                p.dep("ve", ev2)
                ev3 = p.emit("ve", lambda E, c: E.tensor_add(
                    out=c.sb["outt"][:, 0:D], in0=c.sb["outt"][:, 0:D],
                    in1=c.sb["xin"][:, 0:D]), inc="v")
                last = ev3
            else:
                last = ev2
            p.dep("ge", last)
            oname = "y1o" if stage == 1 else "outr"
            p.emit("ge", lambda E, c, t=t, oname=oname: E.dma_start(
                out=c.dram[oname][128 * t:128 * (t + 1), :], in_=c.sb["outt"][:, 0:D]),
                inc="dg", inc_by=16)

    p.wait("ge", "dg")
    p.wait("sy", "dg")

    nc = p.build(nc, sbufs, psums, c)
    return nc


def host_CD_consts(lo):
    DIM_HEAD, WINDOW, N_ = 64, 512, 4096
    half_idx = np.arange(0, DIM_HEAD, 2, dtype=np.float64)
    inv_freq = 1.0 / 10000.0 ** (half_idx / DIM_HEAD)
    gpos = np.arange(lo - 640, lo + 2048, dtype=np.float64)
    freqs = gpos[:, None] * inv_freq[None, :]
    freqs = np.concatenate([freqs, freqs], axis=-1)
    base = (half_idx + 0.4 * DIM_HEAD) / (1.4 * DIM_HEAD)
    power = (gpos - N_ // 2) / float(WINDOW // 2)
    scale = base[None, :] ** power[:, None]
    scale = np.concatenate([scale, scale], axis=-1)
    cos, sin = np.cos(freqs), np.sin(freqs)
    QK = 8.0
    c1q = (cos * scale * QK).astype(np.float32)
    c2q = (sin * scale * QK).astype(np.float32)
    c1k = (cos / scale).astype(np.float32)
    c2k = (sin / scale).astype(np.float32)
    masks = np.zeros((6, 128, 640), dtype=np.float32)
    ss = np.arange(128)[:, None]
    tt = np.arange(640)[None, :]
    gen = (tt >= ss) & (tt <= ss + 512)
    for v in range(6):
        ci = 4 + v
        kglob = lo - 640 + 128 * (ci - 4) + tt
        ok = gen & ((kglob >= 0) | (v == 0))
        masks[v] = np.where(ok, 0.0, -60000.0)
    return c1q, c2q, c1k, c2k, masks.reshape(6 * 128, 640)


def host_CD_inputs(stage, layer_ins, x2=None, p0=None, p1=None, yin=None, x3=None):
    wqkv, q_scale, k_scale, wo, ln_g, ln_b, w1, b1, w2, b2 = layer_ins
    maps = []
    for cid in range(NC):
        b = cid // 2
        lo = (cid % 2) * 2048
        c1q, c2q, c1k, c2k, masks = host_CD_consts(lo)
        m = {
            "wqkv": np.ascontiguousarray(wqkv.astype(np.float32)),
            "qs": np.ascontiguousarray(q_scale.astype(np.float32).reshape(1, 64)),
            "ks": np.ascontiguousarray(k_scale.astype(np.float32).reshape(1, 64)),
            "wol": np.ascontiguousarray(wo.astype(np.float32)),
            "lg": np.ascontiguousarray(ln_g.astype(np.float32).reshape(1, D)),
            "lb": np.ascontiguousarray(ln_b.astype(np.float32).reshape(1, D)),
            "w1": np.ascontiguousarray(w1.astype(np.float32)),
            "b1": np.ascontiguousarray(b1.astype(np.float32).reshape(1, 5460)),
            "w2": np.ascontiguousarray(w2.astype(np.float32)),
            "b2": np.ascontiguousarray(b2.astype(np.float32).reshape(1, D)),
            "c1q": c1q, "c2q": c2q, "c1k": c1k, "c2k": c2k,
            "amask": masks,
            "zrow1": np.zeros((1, D), dtype=np.float32),
        }

        def rows(arr):
            out = np.zeros((LOC, D), dtype=np.float32)
            g0 = lo - 640
            s0 = max(0, g0)
            out[s0 - g0:, :] = arr[s0:lo + 2048, :]
            return out

        if stage == 1:
            m["x2r"] = rows(x2[b])
            m["p0r"] = rows(p0[b])
            m["p1r"] = rows(p1[b])
        else:
            m["yinr"] = rows(yin[b])
            m["x3r"] = np.ascontiguousarray(x3[b][lo:lo + 2048, :].astype(np.float32))
        maps.append(m)
    return maps


def run_C(layer_ins, x2, p0, p1):
    nc = build_CD(1)
    res = run_bass_kernel_spmd(nc, host_CD_inputs(1, layer_ins, x2=x2, p0=p0, p1=p1),
                               core_ids=list(range(NC)))
    x3 = np.zeros((B, S, D), dtype=np.float32)
    y1 = np.zeros((B, S, D), dtype=np.float32)
    for cid in range(NC):
        b, lo = cid // 2, (cid % 2) * 2048
        x3[b, lo:lo + 2048] = res.results[cid]["x3o"]
        y1[b, lo:lo + 2048] = res.results[cid]["y1o"]
    return x3, y1


def run_D(layer_ins, y1, x3):
    nc = build_CD(2)
    res = run_bass_kernel_spmd(nc, host_CD_inputs(2, layer_ins, yin=y1, x3=x3),
                               core_ids=list(range(NC)))
    out = np.zeros((B, S, D), dtype=np.float32)
    for cid in range(NC):
        b, lo = cid // 2, (cid % 2) * 2048
        out[b, lo:lo + 2048] = res.results[cid]["outr"]
    return out


def host_A2_inputs(x1, w_fc, b_fc, w_proj, b_proj):
    x1f = np.ascontiguousarray(x1.reshape(NT, D))
    maps = []
    for cid in range(NC):
        maps.append({
            "x1r": np.ascontiguousarray(x1f[2048 * cid:2048 * (cid + 1)]),
            "wfc": np.ascontiguousarray(w_fc.astype(np.float32)),
            "bfc": np.ascontiguousarray(b_fc.astype(np.float32).reshape(1, 4096)),
            "wproj": np.ascontiguousarray(w_proj.astype(np.float32)),
            "bproj": np.ascontiguousarray(b_proj.astype(np.float32).reshape(1, D)),
        })
    return maps


def run_A2(x1, w_fc, b_fc, w_proj, b_proj):
    nc = build_A2()
    res = run_bass_kernel_spmd(nc, host_A2_inputs(x1, w_fc, b_fc, w_proj, b_proj),
                               core_ids=list(range(NC)))
    out = np.concatenate([res.results[cid]["x2r"] for cid in range(NC)], axis=0)
    return out.reshape(B, S, D)


def host_A1_inputs(x):
    xf = np.ascontiguousarray(x.reshape(NT, D))
    maps = []
    for cid in range(NC):
        i0 = 512 * cid
        isel = np.zeros((S, 512), dtype=BF)
        isel[np.arange(i0, i0 + 512), np.arange(512)] = 1
        xsl = np.ascontiguousarray(x[:, i0:i0 + 512, :].reshape(B * 512, D))
        maps.append({"xfull": xf, "xsl": xsl, "isel": isel})
    return maps


def host_A1_gather(results):
    out = np.zeros((B, S, D), dtype=np.float32)
    for cid in range(NC):
        i0 = 512 * cid
        out[:, i0:i0 + 512, :] = results[cid]["x1sl"].reshape(B, 512, D)
    return out


def run_A1(x):
    nc = build_A1()
    res = run_bass_kernel_spmd(nc, host_A1_inputs(x), core_ids=list(range(NC)))
    return host_A1_gather(res.results)


def kernel(**inputs):
    g = lambda k: np.asarray(inputs[k], dtype=np.float32)
    x = g("L0_attn_x")
    x1 = run_A1(x)
    x2 = run_A2(x1, g("L0_ffn_w_fc"), g("L0_ffn_b_fc"), g("L0_ffn_w_proj"), g("L0_ffn_b_proj"))
    parts = run_B(x2, g("L1_attn_Wq"), g("L1_attn_Wk"), g("L1_attn_Wv"), g("L1_attn_Wg"),
                  g("L1_attn_Wo"), g("L1_attn_gn_w"), g("L1_attn_gn_b"))
    p0 = np.stack([parts[2 * b] for b in range(B)])
    p1 = np.stack([parts[2 * b + 1] for b in range(B)])

    def lay(l):
        return (g("L1_ffn_wqkv")[l], g("L1_ffn_q_scale")[l], g("L1_ffn_k_scale")[l],
                g("L1_ffn_wo")[l], g("L1_ffn_ln_g")[l], g("L1_ffn_ln_b")[l],
                g("L1_ffn_w1")[l], g("L1_ffn_b1")[l], g("L1_ffn_w2")[l], g("L1_ffn_b2")[l])

    x3, y1 = run_C(lay(0), x2, p0, p1)
    out = run_D(lay(1), y1, x3)
    return out.astype(np.float32)


if __name__ == "__main__":
    print("kernel module")
